# version 11
# baseline (speedup 1.0000x reference)
import contextlib
import numpy as np

import concourse.bass as bass
import concourse.mybir as mybir
from concourse.bass_utils import run_bass_kernel_spmd
from concourse import dve_ops
from concourse.dve_op import DveOp
from concourse.dve_spec import Spec, Src0, Src1, C0, lower as dve_lower

F32 = mybir.dt.float32
BF16 = mybir.dt.bfloat16
I32 = mybir.dt.int32
AF = mybir.ActivationFunctionType
OP = mybir.AluOpType
AX = mybir.AxisListType


def _register(name, spec_body, ref, victim):
    ex = next((o for o in dve_ops.OPS if o.name == name), None)
    if ex is not None:
        return ex
    spec = Spec(body=spec_body)
    shas = {v: dve_lower(spec, ver=v, perf_en=False, name=name).sha(v)
            for v in ("v3", "v4")}
    op = DveOp(name, spec, subdim=False, uops_sha=shas, reference=ref)
    vic = next(o for o in dve_ops.OPS if o.name == victim)
    idx = dve_ops.OPS.index(vic)
    row = dve_ops._SUB_OPCODE_FOR_NAME[vic.name]
    dve_ops.OPS[idx] = op
    del dve_ops._SUB_OPCODE_FOR_NAME[vic.name]
    dve_ops._SUB_OPCODE_FOR_NAME[name] = row
    dve_ops.CUSTOM_DVE_SPECS.pop(vic.name, None)
    dve_ops.CUSTOM_DVE_SPECS[name] = spec
    dve_ops.CUSTOM_DVE_REFS.pop(vic.name, None)
    dve_ops.CUSTOM_DVE_REFS[name] = op.ref
    return op


MAGIC = 12582912.0
QUANT_FIN = _register(
    "QUANT_FIN", ((Src0 + C0) - C0) * Src1,
    lambda in0, in1, s0, s1, imm2: np.rint(in0) * in1, "DVEHASH32L_BODY")

E = 2048
TC = 2048
MT = 512
NMT = TC // MT
NH = 64
INTER = 4096
CONV = 4352
NSL_E = 16
NSL_I = 32
NSL_C = 34
FFI = 5632
NSL_R = 44
EPS = 1e-5

ENGS = ("sync", "vector", "scalar", "tensor", "gpsimd")
ENG_SEM = {"vector": "vx", "scalar": "sx", "tensor": "px", "gpsimd": "gx"}


class Prog:
    def __init__(self):
        self.items = {e: [] for e in ENGS}
        self.counts = {}
        self.last_wait = {}

    def sem(self, name):
        self.counts[name] = 0

    def tok(self, sem):
        return (sem, self.counts[sem])

    def emit(self, eng, fn, waits=(), inc=None, amt=1):
        rw = []
        flat = []
        for t in waits:
            if t is None:
                continue
            if isinstance(t, list):
                flat.extend(x for x in t if x is not None)
            else:
                flat.append(t)
        for t in flat:
            s, v = t
            if v > self.last_wait.get((eng, s), 0):
                rw.append((s, v))
                self.last_wait[(eng, s)] = v
        tok = None
        if inc is not None:
            self.counts[inc] += amt
            tok = (inc, self.counts[inc])
        self.items[eng].append((rw, fn, inc, amt))
        return tok

    def dma(self, eng, out, in_, waits=(), sem=None):
        assert sem is not None
        return self.emit(eng, lambda e, o=out, i=in_: e.dma_start(out=o, in_=i),
                         waits, inc=sem, amt=16)


DBG_OFFS = {}


class Arena:
    def __init__(self, ap, total, tag=""):
        self.ap = ap
        self.total = total
        self.off = 0
        self.tag = tag

    def alloc(self, cols, p0=0, p=128, name=None):
        assert self.off + cols <= self.total, f"arena overflow {self.off}+{cols}"
        r = self.ap[p0:p0 + p, self.off:self.off + cols]
        DBG_OFFS[(self.tag, self.off)] = cols
        self.off += cols
        return r


def build_nc():
    nc = bass.Bass("TRN2", target_bir_lowering=False, debug=False)

    xT = nc.dram_tensor("xT", [E, 3 + TC], F32, kind="ExternalInput").ap()
    w_in_img = nc.dram_tensor("w_in_img", [67, 128, 2048], F32, kind="ExternalInput").ap()
    w_out_img = nc.dram_tensor("w_out_img", [16, 128, 4096], F32, kind="ExternalInput").ap()
    wg_img = nc.dram_tensor("wg_img", [NSL_R, 128, 2048], F32, kind="ExternalInput").ap()
    wu_img = nc.dram_tensor("wu_img", [NSL_R, 128, 2048], F32, kind="ExternalInput").ap()
    wd_img = nc.dram_tensor("wd_img", [NSL_R, 128, 2048], F32, kind="ExternalInput").ap()
    wd_nat = nc.dram_tensor("wd_nat", [16, 128, FFI], F32, kind="ExternalInput").ap()
    conv_w_i = nc.dram_tensor("conv_w_i", [128, NSL_C * 4], F32, kind="ExternalInput").ap()
    conv_b_i = nc.dram_tensor("conv_b_i", [128, NSL_C], F32, kind="ExternalInput").ap()
    dt_bias_i = nc.dram_tensor("dt_bias_i", [64, 1], F32, kind="ExternalInput").ap()
    a_log_i = nc.dram_tensor("a_log_i", [64, 1], F32, kind="ExternalInput").ap()
    d_cols_i = nc.dram_tensor("d_cols_i", [128, 32], F32, kind="ExternalInput").ap()
    nw_cols_i = nc.dram_tensor("nw_cols_i", [128, 32], F32, kind="ExternalInput").ap()
    tri_i = nc.dram_tensor("tri_i", [128, 128], F32, kind="ExternalInput").ap()
    ident_i = nc.dram_tensor("ident_i", [128, 128], F32, kind="ExternalInput").ap()
    outT = nc.dram_tensor("outT", [E, TC], F32, kind="ExternalOutput").ap()

    qg_scr = nc.dram_tensor("qg_scr", [NSL_R, 128, 2048], BF16, kind="Internal").ap()
    qu_scr = nc.dram_tensor("qu_scr", [NSL_R, 128, 2048], BF16, kind="Internal").ap()
    qd_scr = nc.dram_tensor("qd_scr", [NSL_R, 128, 2048], BF16, kind="Internal").ap()
    aflat = nc.dram_tensor("aflat", [64, MT], F32, kind="Internal").ap()
    row_scr = nc.dram_tensor("row_scr", [16, 128], F32, kind="Internal").ap()

    es = contextlib.ExitStack()
    with es:
        AF32 = es.enter_context(nc.sbuf_tensor("AF32", [128, 26400], F32)).ap()
        AB16 = es.enter_context(nc.sbuf_tensor("AB16", [128, 49700], BF16)).ap()
        AI32 = es.enter_context(nc.sbuf_tensor("AI32", [128, 1536], I32)).ap()
        f32 = Arena(AF32, 26400, "f")
        b16 = Arena(AB16, 49700, "b")
        i32a = Arena(AI32, 1536)

        xbuf = f32.alloc(NSL_E * MT)
        stash = f32.alloc(NSL_C * 3)
        convw = f32.alloc(NSL_C * 4)
        convb = f32.alloc(NSL_C)
        dcols = f32.alloc(32)
        nwcols = f32.alloc(32)
        tri = f32.alloc(128)
        identf = f32.alloc(128)
        dtb_c = f32.alloc(1)
        alog_c = f32.alloc(1)
        ones_c1 = f32.alloc(1)
        ones_t = f32.alloc(128)
        wsg_cols = f32.alloc(NSL_R)
        wsu_cols = f32.alloc(NSL_R)
        wsd_cols = f32.alloc(16)
        b448d = f32.alloc(2048)
        rowreg1 = f32.alloc(MT)
        rowreg2 = f32.alloc(MT)
        rowreg3 = f32.alloc(MT)
        rowreg4 = f32.alloc(MT)
        rowh = f32.alloc(8)
        cvbuf = f32.alloc(515)
        cbuf = f32.alloc(512)
        sgbuf = f32.alloc(512)
        hidbuf = f32.alloc(512)
        hdbuf = f32.alloc(512)
        mdbuf = f32.alloc(384)
        dtar = f32.alloc(512)
        dtar2 = f32.alloc(512)
        dtar3 = f32.alloc(512)
        acum_fm = f32.alloc(512)
        acumT = f32.alloc(256)
        ddtT = f32.alloc(256)
        dtT_tok = f32.alloc(256)
        b448 = f32.alloc(128)
        meanB = f32.alloc(512)
        rstdB = f32.alloc(512)
        bcA = f32.alloc(512)
        bcB = f32.alloc(512)
        scr1 = f32.alloc(512)
        scr2 = f32.alloc(512)
        rmaxb = f32.alloc(512)
        rmnb = f32.alloc(512)
        zqbuf = f32.alloc(512)
        bcA2 = f32.alloc(512)
        bcB2 = f32.alloc(512)
        xhalo_f = f32.alloc(48)
        dumb = f32.alloc(4)
        absb = f32.alloc(512)
        p0w = f32.alloc(512)
        mdbuf_p = [mdbuf, zqbuf[:, 0:384]]
        sgbuf_p = [sgbuf, bcA2]
        hidbuf_p = [hidbuf, rmaxb]
        arow = [rowreg2[64:65, :], rowreg3[64:65, :]]
        assert f32.off <= 26400, f32.off

        xnb = b16.alloc(NSL_E * MT)
        yfb = b16.alloc(NSL_I * MT)
        tqt = AB16[:, 0:NSL_R * MT]
        qxb = b16.alloc(NSL_E * MT)
        wstr = [b16.alloc(4096), b16.alloc(4096), b16.alloc(2048), b16.alloc(2048)]
        onesb_c1 = b16.alloc(1)
        hidTb = b16.alloc(512)
        btokb = b16.alloc(512)
        bbT = b16.alloc(512)
        cbT = b16.alloc(512)
        gmm = b16.alloc(768)
        mt0b = b16.alloc(256)
        mt1b = b16.alloc(128)
        eeb = b16.alloc(256)
        ceb = b16.alloc(256)
        stsb = b16.alloc(64)
        xnhalo = b16.alloc(NSL_E * 3)
        xsqb = b16.alloc(512)
        bdb = b16.alloc(256)
        mt0b_p = [mt0b, qxb[:, 0:256]]
        mt1b_p = [mt1b, qxb[:, 256:384]]
        eeb_p = [eeb, qxb[:, 384:640]]
        ceb_p = [ceb, qxb[:, 640:896]]
        stsb_p = [stsb, qxb[:, 896:960]]
        bdb_p = [bdb, qxb[:, 960:1216]]
        hidTb_p = [hidTb, qxb[:, 1216:1728]]
        scrb = wstr[0][:, 0:2048]
        sqstat = tqt[:, 0:NSL_E * MT]
        assert b16.off <= 49700, b16.off

        it0 = i32a.alloc(512)
        it1 = i32a.alloc(512)
        it2 = i32a.alloc(512)

        pb = [es.enter_context(nc.psum_tensor(f"pb{k}", [128, 512], F32)).ap()
              for k in range(8)]

        p = Prog()
        SEMS = ["vx", "sx", "px", "gx", "s_cst", "s_x", "s_xh",
                "s_w0", "s_w1", "s_w2", "s_w3", "s_q0", "s_q1",
                "s_ps0", "s_ps1", "s_st0", "s_st1", "s_ar0", "s_ar1",
                "s_af", "s_rs", "s_g0", "s_g1", "s_g2", "s_g3"]
        for s in SEMS:
            p.sem(s)
        semh = {s: es.enter_context(nc.semaphore(s)) for s in SEMS}

        last_dumb = [None]

        def V(fn, waits=(), inc=False):
            return p.emit("vector", fn, waits, inc=("vx" if inc else None))

        def vtok(waits=()):
            t = p.emit("vector",
                       lambda e: e.tensor_copy(out=dumb[0:1, 0:1],
                                               in_=dumb[0:1, 0:1]),
                       tuple(waits) + (last_dumb[0],), inc="vx")
            last_dumb[0] = t
            return t

        def Sc(fn, waits=(), inc=False):
            return p.emit("scalar", fn, waits, inc=("sx" if inc else None))

        def T(fn, waits=(), inc=False):
            return p.emit("tensor", fn, waits, inc=("px" if inc else None))

        def G(fn, waits=(), inc=False):
            return p.emit("gpsimd", fn, waits, inc=("gx" if inc else None))

        def fv(ap):
            return ap.bitcast(F32)

        def onesr(base):
            return ones_t[base:base + 1, :]

        psum_free = {k: None for k in range(8)}
        w_free = {0: None, 1: None, 2: None, 3: None}

        def emit_quant(v_ap, out_ap, w, inc=True, vwait=None, vwait2=None):
            bits = v_ap.bitcast(I32)
            ia, ib, ic = it0[:, 0:w], it1[:, 0:w], it2[:, 0:w]
            t0_ = V(lambda e, a=bits, o=ia: e.tensor_scalar(
                out=o, in0=a, scalar1=0x7F800000, scalar2=None,
                op0=OP.bitwise_and), waits=(vwait, vwait2), inc=True)
            t1_ = V(lambda e, a=ia, o=ib: e.tensor_scalar(
                out=o, in0=a, scalar1=121 << 23, scalar2=3 << 23,
                op0=OP.max, op1=OP.subtract), waits=(t0_,), inc=True)
            t1b_ = V(lambda e, o=ic, b=ib: e.tensor_scalar(
                out=o, in0=b, scalar1=254 << 23, scalar2=-1,
                op0=OP.subtract, op1=OP.mult), waits=(t1_,), inc=True)
            t2_ = V(lambda e, a=v_ap, b=fv(ic), o=zqbuf[:, 0:w]:
                    e.tensor_tensor(out=o, in0=a, in1=b, op=OP.mult),
                    waits=(t1b_,), inc=True)
            return V(lambda e, a=zqbuf[:, 0:w], b=fv(ib), o=out_ap:
                     e._custom_dve(QUANT_FIN, out=o, in0=a, in1=b, s0=MAGIC,
                                   s1=0.0, imm2=0.0),
                     waits=(t2_,), inc=True)

        p.dma("gpsimd", convw, conv_w_i, sem="s_cst")
        p.dma("gpsimd", convb, conv_b_i, sem="s_cst")
        p.dma("gpsimd", dcols, d_cols_i, sem="s_cst")
        p.dma("gpsimd", nwcols, nw_cols_i, sem="s_cst")
        p.dma("gpsimd", tri, tri_i, sem="s_cst")
        p.dma("gpsimd", identf, ident_i, sem="s_cst")
        p.dma("gpsimd", dtb_c[0:64, :], dt_bias_i, sem="s_cst")
        tok_consts = p.dma("gpsimd", alog_c[0:64, :], a_log_i, sem="s_cst")
        V(lambda e: e.memset(ones_c1, 1.0))
        V(lambda e: e.memset(it0, 0))
        V(lambda e: e.memset(it1, 0))
        V(lambda e: e.memset(it2, 0))
        V(lambda e: e.memset(scr1, 0.0))
        V(lambda e: e.memset(dumb, 0.0))
        tok_ms = V(lambda e: e.memset(scr2, 0.0), inc=True)
        V(lambda e: e.memset(onesb_c1, 1.0))
        V(lambda e: e.memset(stash, 0.0))
        V(lambda e: e.memset(rmaxb, 0.0))
        tok_ones = V(lambda e: e.memset(ones_t, 1.0), inc=True)

        tok_prev = None
        for hs in range(16):
            t1 = p.dma("sync", xbuf[:, 0:2816], wd_nat[hs][:, 0:2816],
                       waits=(tok_prev,), sem="s_q0")
            t2 = p.dma("sync", xbuf[:, 2816:5632], wd_nat[hs][:, 2816:5632],
                       waits=(tok_prev,), sem="s_q1")
            tr1 = V(lambda e, o=scr1[:, 0:1], a=xbuf[:, 0:2816]: e.tensor_reduce(
                out=o, in_=a, axis=AX.X, op=OP.max, apply_absolute_value=True),
              waits=(t1, tok_ms, tok_prev), inc=True)
            tr2 = V(lambda e, o=scr1[:, 1:2], a=xbuf[:, 2816:5632]: e.tensor_reduce(
                out=o, in_=a, axis=AX.X, op=OP.max, apply_absolute_value=True),
              waits=(t2,), inc=True)
            tok_prev = V(lambda e, o=wsd_cols[:, hs:hs + 1], a=scr1[:, 0:1],
                         b=scr1[:, 1:2]: e.tensor_tensor(
                out=o, in0=a, in1=b, op=OP.max), waits=(tr1, tr2), inc=True)
        trc = V(lambda e, o=scr1[:, 0:16], a=wsd_cols: e.reciprocal(
            out=o, in_=a), waits=(tok_prev,), inc=True)
        t_rd = V(lambda e, o=scr1[:, 0:16]: e.tensor_scalar(
            out=o, in0=o, scalar1=448.0, scalar2=None, op0=OP.mult),
            waits=(trc,), inc=True)
        V(lambda e, o=wsd_cols: e.tensor_scalar(
            out=o, in0=o, scalar1=1.0 / 448.0, scalar2=None, op0=OP.mult),
          waits=(trc,))
        t_tr = T(lambda e: e.transpose(pb[0][0:16, 0:128], scr1[:, 0:16], identf),
                 waits=(t_rd, tok_consts), inc=True)
        t_ev = V(lambda e: e.tensor_copy(out=scr2[0:16, 0:128],
                                         in_=pb[0][0:16, 0:128]),
                 waits=(t_tr,), inc=True)
        t_b = p.dma("gpsimd", row_scr, scr2[0:16, 0:128], waits=(t_ev,), sem="s_rs")
        t_kb = None
        for q in range(4):
            t_b2 = p.dma("gpsimd",
                         arow[0].rearrange("q (a b) -> q a b", b=128),
                         row_scr[4 * q:4 * q + 4, :].unsqueeze(0),
                         waits=(t_b, t_kb), sem="s_ar0")
            t_mm = T(lambda e, rhs=arow[0], o=pb[1]: e.matmul(
                o[:, 0:512], onesr(64), rhs, start=True, stop=True),
                waits=(t_b2, tok_ones), inc=True)
            t_kb = V(lambda e, o=b448d[:, 512 * q:512 * q + 512], a=pb[1]:
                     e.tensor_copy(out=o, in_=a[:, 0:512]), waits=(t_mm,),
                     inc=True)

        def phase0_img(img, scr, ws_cols, per_slab_scale, prev_store,
                       buf_free, carry):
            for s in range(NSL_R):
                par = s % 2
                st32 = xbuf[:, 2048 * par:2048 * par + 2048]
                t_ld = p.dma("sync", st32, img[s], waits=(buf_free[par],),
                             sem=("s_q0" if par == 0 else "s_q1"))
                if per_slab_scale:
                    t_abp = carry.get("stats")
                    accT = None
                    for ckk in range(4):
                        t_ab = Sc(lambda e, o=absb,
                                  a=st32[:, 512 * ckk:512 * ckk + 512]:
                                  e.activation(out=o, in_=a, func=AF.Abs),
                                  waits=(t_ld, t_abp), inc=True)
                        t_f1 = V(lambda e, o=absb[:, 0:256], a=absb[:, 0:256],
                                 b=absb[:, 256:512]: e.tensor_tensor(
                                     out=o, in0=o, in1=b, op=OP.max),
                                 waits=(t_ab,), inc=True)
                        if ckk == 0:
                            accT = V(lambda e, o=scr1[:, 0:128],
                                     a=absb[:, 0:128], b=absb[:, 128:256]:
                                     e.tensor_tensor(out=o, in0=a, in1=b,
                                                     op=OP.max),
                                     waits=(t_f1, carry.get("tr")), inc=True)
                        else:
                            t_f2 = V(lambda e, o=absb[:, 0:128],
                                     a=absb[:, 0:128], b=absb[:, 128:256]:
                                     e.tensor_tensor(out=o, in0=o, in1=b,
                                                     op=OP.max),
                                     waits=(t_f1,), inc=True)
                            accT = V(lambda e, o=scr1[:, 0:128],
                                     a=absb[:, 0:128]: e.tensor_tensor(
                                         out=o, in0=o, in1=a, op=OP.max),
                                     waits=(t_f2, accT), inc=True)
                        t_abp = accT
                    carry["stats"] = accT
                    t_t = accT
                    t_tr2 = T(lambda e: e.transpose(pb[0][:, 0:128],
                                                    scr1[:, 0:128], identf),
                              waits=(t_t, psum_free[0]), inc=True)
                    carry["tr"] = t_tr2
                    twr = V(lambda e, o=ws_cols[:, s:s + 1], a=pb[0][:, 0:128]:
                            e.tensor_reduce(out=o, in_=a, axis=AX.X, op=OP.max),
                            waits=(t_tr2,), inc=True)
                    twr2 = V(lambda e, o=scr2[:, 0:1], a=ws_cols[:, s:s + 1]:
                             e.reciprocal(out=o, in_=a), waits=(twr,), inc=True)
                    V(lambda e, o=ws_cols[:, s:s + 1]: e.tensor_scalar(
                        out=o, in0=o, scalar1=1.0 / 448.0, scalar2=None,
                        op0=OP.mult), waits=(twr2,))
                    t_rc = V(lambda e, o=scr2[:, 0:1]: e.tensor_scalar(
                        out=o, in0=o, scalar1=448.0, scalar2=None,
                        op0=OP.mult), waits=(twr2,), inc=True)
                    psum_free[0] = t_rc
                    t_tr3 = T(lambda e: e.transpose(pb[0][0:1, 0:128],
                                                    scr2[:, 0:1], identf),
                              waits=(t_rc, psum_free[0]), inc=True)
                    t_ev2 = V(lambda e: e.tensor_copy(
                        out=scr2[0:1, 0:128], in_=pb[0][0:1, 0:128]),
                        waits=(t_tr3,), inc=True)
                    t_mm2 = T(lambda e: e.matmul(
                        pb[1][:, 0:128], onesr(0)[:, 0:128], scr2[0:1, 0:128],
                        start=True, stop=True), waits=(t_ev2, psum_free[1]),
                        inc=True)
                    t_b448 = V(lambda e: e.tensor_copy(out=b448,
                                                       in_=pb[1][:, 0:128]),
                               waits=(t_mm2,), inc=True)
                    psum_free[1] = t_b448
                else:
                    t_b448 = t_ld
                tok_q = None
                for half in range(4):
                    tG = (t_ld, t_b448, carry.get("qf"), carry.get("b448d"))
                    for i in range(4):
                        k = 4 * half + i
                        src_sc = (b448 if per_slab_scale
                                  else b448d[:, 128 * k:128 * k + 128])
                        tG = (G(lambda e, o=p0w[:, 128 * i:128 * i + 128],
                                a=st32[:, 128 * k:128 * k + 128], b=src_sc:
                                e.tensor_tensor(out=o, in0=a, in1=b,
                                                op=OP.mult),
                                waits=tG, inc=True),)
                    tok_q = emit_quant(p0w,
                                       scrb[:, 512 * half:512 * half + 512],
                                       512, vwait=tG[0],
                                       vwait2=(tok_q if tok_q
                                               else prev_store[0]))
                    carry["qf"] = tok_q
                t_st = p.dma("scalar", scr[s], scrb, waits=(tok_q,),
                             sem=("s_ps0" if par == 0 else "s_ps1"))
                buf_free[par] = t_st
                prev_store[0] = t_st

        prev_store = [None]
        p0_buf_free = [t_kb, t_kb]
        p0carry = {"b448d": t_kb}
        phase0_img(wg_img, qg_scr, wsg_cols, True, prev_store, p0_buf_free,
                   p0carry)
        phase0_img(wu_img, qu_scr, wsu_cols, True, prev_store, p0_buf_free,
                   p0carry)
        phase0_img(wd_img, qd_scr, None, False, prev_store, p0_buf_free,
                   p0carry)
        tok_p0a = p.tok("s_ps0")
        tok_p0b = p.tok("s_ps1")
        tok_p0 = tok_p0a
        w_free = {0: tok_p0a, 1: tok_p0b, 2: tok_p0b, 3: tok_p0a}

        def ln_stats(src, cast_dst, sq_dst, mean_row, rstd_row, wait_tok,
                     need_rstd_b):
            t_c = V(lambda e, a=src, o=cast_dst: e.tensor_copy(out=o, in_=a),
                    waits=(wait_tok,), inc=True)
            t_sq = Sc(lambda e, a=cast_dst, o=sq_dst: e.activation(
                out=o, in_=a, func=AF.Square), waits=(t_c,), inc=True)
            for k in range(NSL_E):
                T(lambda e, r=cast_dst[:, MT * k:MT * k + MT], k=k: e.matmul(
                    pb[5][0:1, 0:MT], onesb_c1, r, start=(k == 0),
                    stop=(k == NSL_E - 1)), waits=(t_c, psum_free[5]))
            t_s2 = None
            for k in range(NSL_E):
                t_s2 = T(lambda e, r=sq_dst[:, MT * k:MT * k + MT], k=k:
                         e.matmul(pb[5][32:33, 0:MT], onesb_c1, r,
                                  start=(k == 0), stop=(k == NSL_E - 1)),
                         waits=(t_sq,), inc=(k == NSL_E - 1))
            tm_ = V(lambda e, o=mean_row, a=pb[5][0:1, 0:MT]: e.tensor_scalar(
                out=o, in0=a, scalar1=1.0 / E, scalar2=None, op0=OP.mult),
              waits=(t_s2,), inc=True)
            te_ = V(lambda e, o=rstd_row, a=pb[5][32:33, 0:MT]: e.tensor_scalar(
                out=o, in0=a, scalar1=1.0 / E, scalar2=None, op0=OP.mult),
                inc=True)
            tq_ = V(lambda e, o=scr1[0:1, 0:MT], a=mean_row: e.tensor_tensor(
                out=o, in0=a, in1=a, op=OP.mult), waits=(tm_,), inc=True)
            ts_ = V(lambda e, o=rstd_row, a=scr1[0:1, 0:MT]: e.tensor_tensor(
                out=o, in0=o, in1=a, op=OP.subtract), waits=(te_, tq_),
                inc=True)
            t_v = V(lambda e, o=rstd_row: e.tensor_scalar(
                out=o, in0=o, scalar1=EPS, scalar2=None, op0=OP.add),
                waits=(ts_,), inc=True)
            psum_free[5] = t_v
            t_sq2 = Sc(lambda e, o=rstd_row: e.activation(
                out=o, in_=o, func=AF.Sqrt), waits=(t_v,), inc=True)
            t_rp = V(lambda e, o=rstd_row: e.reciprocal(out=o, in_=o),
                     waits=(t_sq2,), inc=True)
            t_m1 = T(lambda e, r=mean_row: e.matmul(
                pb[6][:, 0:MT], onesr(0), r, start=True, stop=True),
                waits=(t_rp, psum_free[6]), inc=True)
            t_e1 = V(lambda e, o=meanB: e.tensor_copy(out=o, in_=pb[6][:, 0:MT]),
                     waits=(t_m1,), inc=True)
            psum_free[6] = t_e1
            t_e2 = t_e1
            if need_rstd_b:
                t_m2 = T(lambda e, r=rstd_row: e.matmul(
                    pb[7][:, 0:MT], onesr(0), r, start=True, stop=True),
                    waits=(t_rp, psum_free[7]), inc=True)
                t_e2 = V(lambda e, o=rstdB: e.tensor_copy(
                    out=o, in_=pb[7][:, 0:MT]), waits=(t_m2,), inc=True)
                psum_free[7] = t_e2
            return t_e1, t_e2

        def proj_slab(wb, bank, rhs_b16, halo_bank=None, halo_rhs=None,
                      wait_w=None, wait_rhs=None, wait_halo=None):
            tok = None
            for k in range(NSL_E):
                tok = T(lambda e, l=wb[:, 128 * k:128 * k + 128],
                        r=rhs_b16[:, MT * k:MT * k + MT], k=k: e.matmul(
                            pb[bank][:, 0:MT], l, r, start=(k == 0),
                            stop=(k == NSL_E - 1)),
                        waits=(wait_w, wait_rhs, psum_free[bank]),
                        inc=(k == NSL_E - 1))
            tokh = None
            if halo_bank is not None:
                for k in range(NSL_E):
                    tokh = T(lambda e, l=wb[:, 128 * k:128 * k + 128],
                             r=halo_rhs[:, 3 * k:3 * k + 3], k=k: e.matmul(
                                 pb[halo_bank][:, 0:3], l, r, start=(k == 0),
                                 stop=(k == NSL_E - 1)),
                             waits=(psum_free[halo_bank], wait_halo),
                             inc=(k == NSL_E - 1))
            return tok, tokh

        cv_free = [None]

        def conv_silu(csl, bank, out_f32, tok_mm, tok_halo, halo_from_psum,
                      halo_bank):
            if halo_from_psum:
                t_s0 = V(lambda e, o=cvbuf[:, 0:3], a=pb[halo_bank][:, 0:3]:
                         e.tensor_copy(out=o, in_=a),
                         waits=(tok_halo, cv_free[0]), inc=True)
                psum_free[halo_bank] = t_s0
            else:
                t_s0 = G(lambda e, o=cvbuf[:, 0:3],
                         a=stash[:, 3 * csl:3 * csl + 3]:
                         e.tensor_copy(out=o, in_=a), waits=(cv_free[0],),
                         inc=True)
            t_ev = Sc(lambda e, o=cvbuf[:, 3:515], a=pb[bank][:, 0:MT]:
                      e.activation(out=o, in_=a, func=AF.Copy),
                      waits=(tok_mm, cv_free[0]), inc=True)
            psum_free[bank] = t_ev
            t_us = G(lambda e, o=stash[:, 3 * csl:3 * csl + 3],
                     a=cvbuf[:, 512:515]: e.tensor_copy(out=o, in_=a),
                     waits=(t_ev, t_s0), inc=True)
            V(lambda e, o=cbuf, a=cvbuf[:, 0:512],
              w0=convw[:, 4 * csl:4 * csl + 1], b=convb[:, csl:csl + 1]:
              e.tensor_scalar(out=o, in0=a, scalar1=w0, scalar2=b,
                              op0=OP.mult, op1=OP.add), waits=(t_ev, t_s0))
            V(lambda e, o=hdbuf, a=cvbuf[:, 1:513],
              w=convw[:, 4 * csl + 1:4 * csl + 2]:
              e.tensor_scalar(out=o, in0=a, scalar1=w, scalar2=None,
                              op0=OP.mult))
            V(lambda e, o=scr1, a=cvbuf[:, 2:514],
              w=convw[:, 4 * csl + 2:4 * csl + 3]:
              e.tensor_scalar(out=o, in0=a, scalar1=w, scalar2=None,
                              op0=OP.mult))
            t_d4 = V(lambda e, o=scr2, a=cvbuf[:, 3:515],
                     w=convw[:, 4 * csl + 3:4 * csl + 4]:
                     e.tensor_scalar(out=o, in0=a, scalar1=w, scalar2=None,
                                     op0=OP.mult), inc=True)
            t_ab_ = V(lambda e, o=cbuf, a=hdbuf: e.tensor_tensor(
                out=o, in0=o, in1=a, op=OP.add), waits=(t_d4,), inc=True)
            t_cd_ = V(lambda e, o=scr1, a=scr2: e.tensor_tensor(
                out=o, in0=o, in1=a, op=OP.add), inc=True)
            t_cv = V(lambda e, o=cbuf, a=scr1: e.tensor_tensor(
                out=o, in0=o, in1=a, op=OP.add), waits=(t_ab_, t_cd_),
                inc=True)
            cv_free[0] = t_cv
            t_si = Sc(lambda e, o=out_f32, a=cbuf: e.activation(
                out=o, in_=a, func=AF.Silu), waits=(t_cv,), inc=True)
            return t_si

        out_final = None
        tok_x_free = tok_p0
        for m in range(NMT):
            col0 = 3 + MT * m
            t_x = p.dma("gpsimd",
                        xbuf.rearrange("p (s t) -> p s t", t=MT),
                        xT.rearrange("(s p) t -> p s t", p=128)[:, :, col0:col0 + MT],
                        waits=(tok_x_free, tok_p0a, tok_p0b), sem="s_x")
            if m == 0:
                t_xh = p.dma("gpsimd",
                             xhalo_f.rearrange("p (s t) -> p s t", t=3),
                             xT.rearrange("(s p) t -> p s t", p=128)[:, :, 0:3],
                             sem="s_xh")
            t_mB, t_rB = ln_stats(xbuf, qxb, sqstat, rowreg1[0:1, :],
                                  rowreg2[0:1, :], t_x, True)
            tok_xn = None
            for k in range(NSL_E):
                tsub = G(lambda e, a=xbuf[:, MT * k:MT * k + MT], o=bcB:
                         e.tensor_tensor(out=o, in0=a, in1=meanB,
                                         op=OP.subtract),
                         waits=(t_mB, t_rB, tok_xn), inc=True)
                tok_xn = G(lambda e, o=xnb[:, MT * k:MT * k + MT], a=bcB:
                           e.tensor_tensor(out=o, in0=a, in1=rstdB,
                                           op=OP.mult), waits=(tsub,),
                           inc=True)
            if m == 0:
                t_ch = V(lambda e, a=xhalo_f, o=xnhalo: e.tensor_copy(
                    out=o, in_=a), waits=(t_xh,), inc=True)
                t_sqh = Sc(lambda e, a=xnhalo, o=xsqb[:, 0:48]: e.activation(
                    out=o, in_=a, func=AF.Square), waits=(t_ch,), inc=True)
                for k in range(NSL_E):
                    T(lambda e, r=xnhalo[:, 3 * k:3 * k + 3], k=k: e.matmul(
                        pb[5][64:65, 0:3], onesb_c1, r, start=(k == 0),
                        stop=(k == NSL_E - 1)), waits=(t_ch, psum_free[5]))
                t_sh = None
                for k in range(NSL_E):
                    t_sh = T(lambda e, r=xsqb[:, 3 * k:3 * k + 3], k=k:
                             e.matmul(pb[5][64:65, 8:11], onesb_c1, r,
                                      start=(k == 0), stop=(k == NSL_E - 1)),
                             waits=(t_sqh,), inc=(k == NSL_E - 1))
                th1 = V(lambda e, o=rowh[0:1, 0:3], a=pb[5][64:65, 0:3]:
                  e.tensor_scalar(out=o, in0=a, scalar1=1.0 / E, scalar2=None,
                                  op0=OP.mult), waits=(t_sh,), inc=True)
                th2 = V(lambda e, o=rowh[0:1, 4:7], a=pb[5][64:65, 8:11]:
                  e.tensor_scalar(out=o, in0=a, scalar1=1.0 / E, scalar2=None,
                                  op0=OP.mult), inc=True)
                th3 = V(lambda e, o=scr1[0:1, 0:3], a=rowh[0:1, 0:3]:
                  e.tensor_tensor(out=o, in0=a, in1=a, op=OP.mult),
                  waits=(th1,), inc=True)
                th4 = V(lambda e, o=rowh[0:1, 4:7], a=scr1[0:1, 0:3]:
                  e.tensor_tensor(out=o, in0=o, in1=a, op=OP.subtract),
                  waits=(th2, th3), inc=True)
                t_hv = V(lambda e, o=rowh[0:1, 4:7]: e.tensor_scalar(
                    out=o, in0=o, scalar1=EPS, scalar2=None, op0=OP.add),
                    waits=(th4,), inc=True)
                psum_free[5] = t_hv
                t_hs = Sc(lambda e, o=rowh[0:1, 4:7]: e.activation(
                    out=o, in_=o, func=AF.Sqrt), waits=(t_hv,), inc=True)
                t_hr = V(lambda e, o=rowh[0:1, 4:7]: e.reciprocal(
                    out=o, in_=o), waits=(t_hs,), inc=True)
                t_hm1 = T(lambda e: e.matmul(pb[0][:, 0:3], onesr(0),
                                             rowh[0:1, 0:3], start=True,
                                             stop=True),
                          waits=(t_hr, psum_free[0]), inc=True)
                t_hm2 = T(lambda e: e.matmul(pb[1][:, 0:3],
                                             onesr(0),
                                             rowh[0:1, 4:7], start=True,
                                             stop=True), waits=(psum_free[1],),
                          inc=True)
                tok_xnh = None
                for k in range(NSL_E):
                    tsub = V(lambda e, a=xhalo_f[:, 3 * k:3 * k + 3],
                             o=scr1[:, 0:3]:
                             e.tensor_tensor(out=o, in0=a, in1=pb[0][:, 0:3],
                                             op=OP.subtract),
                             waits=(t_hm1, t_hm2, tok_xnh), inc=True)
                    tok_xnh = V(lambda e, o=xnhalo[:, 3 * k:3 * k + 3],
                                a=scr1[:, 0:3]:
                                e.tensor_tensor(out=o, in0=a,
                                                in1=pb[1][:, 0:3],
                                                op=OP.mult), waits=(tsub,),
                                inc=True)
                psum_free[0] = tok_xnh
                psum_free[1] = tok_xnh
            else:
                tok_xnh = None
            tok_bbT = tok_cbT = tok_btok = tok_acumT = tok_ddtT = tok_dtT = tok_afl = None
            for jj, kind in enumerate(["B", "C", "dt"]):
                jsl = 64 + jj
                par = jsl % 2
                t_w = p.dma("gpsimd", wstr[par][:, 0:2048], w_in_img[jsl],
                            waits=(w_free[par],), sem=f"s_w{par}")
                halo = (m == 0 and kind in ("B", "C"))
                t_mm, t_h = proj_slab(wstr[par], 0, xnb,
                                      halo_bank=(3 if halo else None),
                                      halo_rhs=(xnhalo if halo else None),
                                      wait_w=t_w, wait_rhs=tok_xn,
                                      wait_halo=tok_xnh)
                w_free[par] = t_mm
                if kind == "dt":
                    t_ev = V(lambda e, o=dtar[0:64, :], a=pb[0][0:64, 0:MT]:
                             e.tensor_scalar(out=o, in0=a,
                                             scalar1=dtb_c[0:64, :],
                                             scalar2=None, op0=OP.add),
                             waits=(t_mm, tok_consts), inc=True)
                    psum_free[0] = t_ev
                    t_ab = Sc(lambda e, o=dtar2[0:64, :], a=dtar[0:64, :]:
                              e.activation(out=o, in_=a, func=AF.Abs),
                              waits=(t_ev,), inc=True)
                    t_ex = Sc(lambda e, o=dtar2[0:64, :], a=dtar2[0:64, :]:
                              e.activation(out=o, in_=a, func=AF.Exp,
                                           scale=-1.0), waits=(t_ab,),
                              inc=True)
                    t_ln = Sc(lambda e, o=dtar2[0:64, :], a=dtar2[0:64, :]:
                              e.activation(out=o, in_=a, func=AF.Ln,
                                           bias=ones_c1[0:64, :]),
                              waits=(t_ex,), inc=True)
                    t_mx = V(lambda e, o=dtar3[0:64, :], a=dtar[0:64, :]:
                      e.tensor_scalar(out=o, in0=a, scalar1=0.0, scalar2=None,
                                      op0=OP.max), waits=(t_ln,), inc=True)
                    t_dtv = V(lambda e, o=dtar3[0:64, :], a=dtar2[0:64, :]:
                      e.tensor_tensor(out=o, in0=o, in1=a, op=OP.add),
                      waits=(t_mx,), inc=True)
                    t_A = Sc(lambda e, o=scr2[0:64, 0:1], a=alog_c[0:64, :]:
                             e.activation(out=o, in_=a, func=AF.Exp),
                             waits=(tok_consts,), inc=True)
                    t_nA = V(lambda e, o=scr2[0:64, 0:1]: e.tensor_scalar(
                        out=o, in0=o, scalar1=-1.0, scalar2=None, op0=OP.mult),
                      waits=(t_A,), inc=True)
                    t_at = V(lambda e, o=dtar[0:64, :], a=dtar3[0:64, :]:
                      e.tensor_scalar(out=o, in0=a, scalar1=scr2[0:64, 0:1],
                                      scalar2=None, op0=OP.mult),
                      waits=(t_dtv, t_nA), inc=True)
                    tsc = [t_at]
                    for c in range(2):
                        tsc.append(V(lambda e, o=acum_fm[0:64, 256 * c:256 * c + 256],
                          a=dtar[0:64, 256 * c:256 * c + 256]:
                          e.tensor_tensor_scan(out=o, data0=a, data1=a,
                                               initial=0.0, op0=OP.add,
                                               op1=OP.bypass),
                          waits=(t_at,), inc=True))
                    t_ac = vtok(waits=tuple(tsc))
                    for c in range(2):
                        Sc(lambda e, o=dtar2[0:64, 256 * c:256 * c + 256],
                           a=acum_fm[0:64, 256 * c:256 * c + 256],
                           bcol=acum_fm[0:64, 256 * c + 255:256 * c + 256]:
                           e.activation(out=o, in_=a, func=AF.Exp, scale=-1.0,
                                        bias=bcol), waits=(t_ac,))
                    t_dc = Sc(lambda e: e.activation(
                        out=scr2[0:1, 1:2], in_=scr2[0:1, 1:2],
                        func=AF.Copy), inc=True)
                    t_dd = V(lambda e, o=dtar2[0:64, :], a=dtar3[0:64, :]:
                             e.tensor_tensor(out=o, in0=o, in1=a, op=OP.mult),
                             waits=(t_dc, t_dtv), inc=True)
                    for (src, dst) in [(acum_fm, acumT), (dtar2, ddtT),
                                       (dtar3, dtT_tok)]:
                        tt = None
                        for bb in range(4):
                            tt = T(lambda e, a=src[0:64, 128 * bb:128 * bb + 128],
                                   o=pb[2][0:128, 64 * bb:64 * bb + 64]:
                                   e.transpose(o, a, identf[0:64, 0:64]),
                                   waits=(t_dd, psum_free[2]),
                                   inc=(bb == 3))
                        tok_tr = V(lambda e, o=dst, a=pb[2][:, 0:256]:
                                   e.tensor_copy(out=o, in_=a), waits=(tt,),
                                   inc=True)
                        psum_free[2] = tok_tr
                        if src is acum_fm:
                            tok_acumT = tok_tr
                        elif src is dtar2:
                            tok_ddtT = tok_tr
                        else:
                            tok_dtT = tok_tr
                    tok_afl = p.dma("gpsimd", aflat, acum_fm[0:64, :],
                                    waits=(t_ac, p.tok("s_ar0"),
                                           p.tok("s_ar1")), sem="s_af")
                else:
                    csl = 32 + jj
                    t_si = conv_silu(csl, 0, hidbuf, t_mm, t_h,
                                     halo_from_psum=(m == 0), halo_bank=3)
                    dst = bbT if kind == "B" else cbT
                    tok_bc = V(lambda e, o=dst, a=hidbuf: e.tensor_copy(
                        out=o, in_=a), waits=(t_si,), inc=True)
                    if kind == "B":
                        tok_bbT = tok_bc
                        t_bt = None
                        for bb in range(4):
                            t_bt = T(lambda e, a=hidbuf[:, 128 * bb:128 * bb + 128],
                                     o=pb[2][:, 128 * bb:128 * bb + 128]:
                                     e.transpose(o, a, identf),
                                     waits=(t_si, psum_free[2]),
                                     inc=(bb == 3))
                        tok_btok = V(lambda e, o=btokb, a=pb[2][:, 0:512]:
                                     e.tensor_copy(out=o, in_=a),
                                     waits=(t_bt,), inc=True)
                        psum_free[2] = tok_btok
                    else:
                        tok_cbT = tok_bc
            for c in range(2):
                for sb in range(2):
                    lw = 256 if sb == 0 else 128
                    l0 = 256 * c + 128 * sb
                    t_g = T(lambda e, l=bbT[:, 256 * c + 128 * sb:256 * c + 128 * sb + 128],
                            r=cbT[:, l0:l0 + lw], lw=lw: e.matmul(
                                pb[3][:, 0:lw], l, r, start=True, stop=True),
                            waits=(tok_bbT, tok_cbT, psum_free[3]), inc=True)
                    off = 384 * c + 256 * sb
                    V(lambda e, o=gmm[:, off:off + 128], a=pb[3][:, 0:128]:
                      e.tensor_tensor(out=o, in0=a, in1=tri, op=OP.mult),
                      waits=(t_g, tok_consts))
                    if sb == 0:
                        V(lambda e, o=gmm[:, off + 128:off + 256],
                          a=pb[3][:, 128:256]: e.tensor_copy(out=o, in_=a))
                    psum_free[3] = vtok()
            tok_gmm = p.tok("vx")

            tok_rmsacc = [V(lambda e: e.memset(rowreg3[0:1, :], 0.0),
                            inc=True)]
            arow_free = [None, None]
            prev_iter = [None, None]
            fs = [dict() for _ in range(NSL_I)]
            sg_free = [None, None]
            hd_free = [None]
            rs_prev = [None]
            halo = (m == 0)

            def em_dma(j):
                parA, parB = j % 2, 2 + j % 2
                fs[j]["wg"] = p.dma("gpsimd", wstr[parA][:, 0:2048],
                                    w_in_img[j], waits=(w_free[parA],),
                                    sem=f"s_w{parA}")
                fs[j]["wh"] = p.dma("gpsimd", wstr[parB], w_in_img[32 + j],
                                    waits=(w_free[parB],), sem=f"s_w{parB}")

            def em_gate(j):
                parA = j % 2
                t_gm, _ = proj_slab(wstr[parA], 0, xnb, wait_w=fs[j]["wg"],
                                    wait_rhs=tok_xn)
                w_free[parA] = t_gm
                fs[j]["gm"] = t_gm

            def em_gsilu(j):
                t_sg = Sc(lambda e, o=sgbuf_p[j % 2], a=pb[0][:, 0:MT]:
                          e.activation(out=o, in_=a, func=AF.Silu),
                          waits=(fs[j]["gm"], sg_free[j % 2]), inc=True)
                psum_free[0] = t_sg
                fs[j]["sg"] = t_sg

            def em_hid(j):
                parB = 2 + j % 2
                t_hmm, t_hh = proj_slab(wstr[parB], 0, xnb,
                                        halo_bank=(3 if halo else None),
                                        halo_rhs=(xnhalo if halo else None),
                                        wait_w=fs[j]["wh"], wait_rhs=tok_xn,
                                        wait_halo=(tok_xnh if halo else None))
                w_free[parB] = t_hmm
                fs[j]["hmm"] = t_hmm
                fs[j]["hh"] = t_hh

            def em_conv(j):
                if halo:
                    t_s0 = V(lambda e, o=cvbuf[:, 0:3], a=pb[3][:, 0:3]:
                             e.tensor_copy(out=o, in_=a),
                             waits=(fs[j]["hh"], cv_free[0]), inc=True)
                    psum_free[3] = t_s0
                else:
                    t_s0 = G(lambda e, o=cvbuf[:, 0:3],
                             a=stash[:, 3 * j:3 * j + 3]:
                             e.tensor_copy(out=o, in_=a), waits=(cv_free[0],),
                             inc=True)
                t_ev = Sc(lambda e, o=cvbuf[:, 3:515], a=pb[0][:, 0:MT]:
                          e.activation(out=o, in_=a, func=AF.Copy),
                          waits=(fs[j]["hmm"], cv_free[0]), inc=True)
                psum_free[0] = t_ev
                G(lambda e, o=stash[:, 3 * j:3 * j + 3],
                  a=cvbuf[:, 512:515]: e.tensor_copy(out=o, in_=a),
                  waits=(t_ev, t_s0))
                V(lambda e, o=cbuf, a=cvbuf[:, 0:512],
                  w0=convw[:, 4 * j:4 * j + 1], b=convb[:, j:j + 1]:
                  e.tensor_scalar(out=o, in0=a, scalar1=w0, scalar2=b,
                                  op0=OP.mult, op1=OP.add), waits=(t_ev, t_s0))
                V(lambda e, o=bcB, a=cvbuf[:, 1:513],
                  w=convw[:, 4 * j + 1:4 * j + 2]:
                  e.tensor_scalar(out=o, in0=a, scalar1=w, scalar2=None,
                                  op0=OP.mult))
                V(lambda e, o=scr1, a=cvbuf[:, 2:514],
                  w=convw[:, 4 * j + 2:4 * j + 3]:
                  e.tensor_scalar(out=o, in0=a, scalar1=w, scalar2=None,
                                  op0=OP.mult))
                t_d4 = V(lambda e, o=scr2, a=cvbuf[:, 3:515],
                         w=convw[:, 4 * j + 3:4 * j + 4]:
                         e.tensor_scalar(out=o, in0=a, scalar1=w, scalar2=None,
                                         op0=OP.mult), inc=True)
                t_ab_ = V(lambda e, o=cbuf, a=bcB: e.tensor_tensor(
                    out=o, in0=o, in1=a, op=OP.add), waits=(t_d4,), inc=True)
                t_cd_ = V(lambda e, o=scr1, a=scr2: e.tensor_tensor(
                    out=o, in0=o, in1=a, op=OP.add), inc=True)
                t_cv = V(lambda e, o=cbuf, a=scr1: e.tensor_tensor(
                    out=o, in0=o, in1=a, op=OP.add), waits=(t_ab_, t_cd_),
                    inc=True)
                cv_free[0] = t_cv
                fs[j]["cv"] = t_cv

            def em_csilu(j):
                fs[j]["hid"] = Sc(lambda e, o=hidbuf_p[j % 2], a=cbuf:
                                  e.activation(out=o, in_=a, func=AF.Silu),
                                  waits=(fs[j]["cv"],), inc=True)

            def em_tr(j):
                t_ht = None
                for bb in range(4):
                    t_ht = T(lambda e, a=hidbuf_p[j % 2][:, 128 * bb:128 * bb + 128],
                             o=pb[3][:, 128 * bb:128 * bb + 128]:
                             e.transpose(o, a, identf),
                             waits=(fs[j]["hid"], psum_free[3]), inc=(bb == 3))
                tok_hidT = V(lambda e, o=hidTb_p[j % 2], a=pb[3][:, 0:512]:
                             e.tensor_copy(out=o, in_=a), waits=(t_ht,),
                             inc=True)
                psum_free[3] = tok_hidT
                fs[j]["hT"] = tok_hidT

            def em_scan_pre(j, c):
                t_ar = p.dma("gpsimd", arow[c],
                             aflat[2 * j:2 * j + 2, 256 * c:256 * c + 256].unsqueeze(0),
                             waits=(tok_afl, arow_free[c]),
                             sem=f"s_ar{c}")
                t_bc = T(lambda e, r=arow[c]: e.matmul(
                    pb[4][:, 0:512], onesr(64), r, start=True, stop=True),
                    waits=(t_ar,) + tuple(psum_free[4]
                                          if isinstance(psum_free[4], list)
                                          else [psum_free[4]]), inc=True)
                arow_free[c] = t_bc
                psum_free[4] = []
                fs[j]["bc%d" % c] = t_bc

            def em_scan_it(j, c, hh):
                t_bc = fs[j]["bc%d" % c]
                hidTx = hidTb_p[j % 2]
                it = 2 * c + hh
                ip = it % 2
                mdX = mdbuf_p[ip]
                mt0X = mt0b_p[ip]
                mt1X = mt1b_p[ip]
                eeX = eeb_p[ip]
                ceX = ceb_p[ip]
                stX = stsb_p[ip]
                bdX = bdb_p[ip]
                sbank = 3 if ip == 0 else 5
                hcol = 256 * hh
                hg = 2 * j + hh
                acol0 = acumT[:, 64 * (2 * c) + hg:64 * (2 * c) + hg + 1]
                acol1 = acumT[:, 64 * (2 * c + 1) + hg:64 * (2 * c + 1) + hg + 1]
                V(lambda e, o=mdX[:, 0:256], a=pb[4][:, hcol:hcol + 256],
                  s=acol0: e.tensor_scalar(
                      out=o, in0=a, scalar1=s, scalar2=0.0,
                      op0=OP.subtract, op1=OP.min),
                  waits=(t_bc, tok_acumT, prev_iter[ip]))
                V(lambda e, o=mdX[:, 256:384],
                  a=pb[4][:, hcol + 128:hcol + 256], s=acol1:
                  e.tensor_scalar(out=o, in0=a, scalar1=s, scalar2=0.0,
                                  op0=OP.subtract, op1=OP.min))
                t_md = vtok()
                t_me = Sc(lambda e, o=mt0X, a=mdX[:, 0:256]:
                          e.activation(out=o, in_=a, func=AF.Exp),
                          waits=(t_md,), inc=True)
                V(lambda e, o=mt0X, a=mt0X,
                  g=gmm[:, 384 * c:384 * c + 256],
                  s=dtT_tok[:, 64 * (2 * c) + hg:64 * (2 * c) + hg + 1]:
                  e.scalar_tensor_tensor(
                      out=o, in0=a, scalar=s, in1=g, op0=OP.mult,
                      op1=OP.mult),
                  waits=(t_me, tok_gmm, tok_dtT))
                t_me1 = Sc(lambda e, o=mt1X, a=mdX[:, 256:384]:
                           e.activation(out=o, in_=a, func=AF.Exp),
                           inc=True)
                t_mt = V(lambda e, o=mt1X, a=mt1X,
                         g=gmm[:, 384 * c + 256:384 * c + 384],
                         s=dtT_tok[:, 64 * (2 * c + 1) + hg:
                                   64 * (2 * c + 1) + hg + 1]:
                         e.scalar_tensor_tensor(
                             out=o, in0=a, scalar=s, in1=g,
                             op0=OP.mult, op1=OP.mult),
                         waits=(t_me1, tok_dtT), inc=True)
                t_ee = Sc(lambda e, o=eeX, a=pb[4][:, hcol:hcol + 256]:
                          e.activation(out=o, in_=a, func=AF.Exp),
                          waits=(t_bc, prev_iter[ip]), inc=True)
                t_ce = G(lambda e, o=ceX, a=cbT[:, 256 * c:256 * c + 256],
                         b=eeX: e.tensor_tensor(out=o, in0=a, in1=b,
                                                op=OP.mult),
                         waits=(t_ee, tok_cbT), inc=True)
                for sb in range(2):
                    bcol = 64 * (2 * c + sb) + hg
                    V(lambda e, o=bdX[:, 128 * sb:128 * sb + 128],
                      a=btokb[:, 128 * (2 * c + sb):128 * (2 * c + sb) + 128],
                      s=ddtT[:, bcol:bcol + 1]: e.tensor_scalar(
                          out=o, in0=a, scalar1=s, scalar2=None,
                          op0=OP.mult),
                      waits=(tok_btok, tok_ddtT, prev_iter[ip]))
                t_bd = vtok()
                t_st = None
                for sb in range(2):
                    t_st = T(lambda e, l=bdX[:, 128 * sb:128 * sb + 128],
                             r=hidTx[:, 128 * (2 * c + sb) + 64 * hh:
                                     128 * (2 * c + sb) + 64 * hh + 64],
                             sb=sb: e.matmul(pb[sbank][0:128, 0:64], l, r,
                                             start=(sb == 0),
                                             stop=(sb == 1)),
                             waits=(t_bd, fs[j]["hT"], psum_free[sbank]),
                             inc=(sb == 1))
                t_sev = Sc(lambda e, o=stX, a=pb[sbank][0:128, 0:64]:
                           e.activation(out=o, in_=a, func=AF.Copy),
                           waits=(t_st,), inc=True)
                psum_free[sbank] = t_sev
                r0 = 64 * hh
                qb = {(0, 0): 1, (0, 1): 2, (1, 0): 6, (1, 1): 7}[(c, hh)]
                yreg = pb[qb][r0:r0 + 64, 0:256]
                T(lambda e, l=hidTx[:, 128 * (2 * c) + 64 * hh:
                                    128 * (2 * c) + 64 * hh + 64],
                  r=mt0X, o=yreg:
                  e.matmul(o, l, r, start=True, stop=False),
                  waits=(fs[j]["hT"], t_mt, psum_free[qb]))
                T(lambda e, l=hidTx[:, 128 * (2 * c + 1) + 64 * hh:
                                    128 * (2 * c + 1) + 64 * hh + 64],
                  r=mt1X, o=pb[qb][r0:r0 + 64, 128:256]: e.matmul(
                      o, l, r, start=False, stop=False))
                tok_yp = T(lambda e, l=stX, r=ceX, o=yreg:
                           e.matmul(o, l, r, start=False, stop=True),
                           waits=(t_sev, t_ce), inc=True)
                fs[j].setdefault("yparts", []).append(tok_yp)
                prev_iter[ip] = tok_yp
                psum_free[4] = psum_free[4] + [t_ee, t_md]

            def em_epi(j):
                t_hd = V(lambda e, o=hdbuf, a=hidbuf_p[j % 2],
                         s=dcols[:, j:j + 1]:
                         e.tensor_scalar(out=o, in0=a, scalar1=s, scalar2=None,
                                         op0=OP.mult),
                         waits=(fs[j]["hid"], tok_consts, hd_free[0]),
                         inc=True)
                t_y1s = []
                for (cq, hq), qb in {(0, 0): 1, (0, 1): 2, (1, 0): 6,
                                     (1, 1): 7}.items():
                    r0q = 64 * hq
                    t_y1 = V(lambda e, o=hdbuf[r0q:r0q + 64,
                                              256 * cq:256 * cq + 256],
                             a=pb[qb][r0q:r0q + 64, 0:256]:
                             e.tensor_tensor(out=o, in0=a, in1=o, op=OP.add),
                             waits=tuple(fs[j]["yparts"]) + (t_hd,),
                             inc=True)
                    psum_free[qb] = t_y1
                    t_y1s.append(t_y1)
                t_yf = G(lambda e, o=yfb[:, MT * j:MT * j + MT], a=hdbuf,
                         b=sgbuf_p[j % 2]: e.tensor_tensor(
                             out=o, in0=a, in1=b, op=OP.mult),
                         waits=tuple(t_y1s) + (fs[j]["sg"],), inc=True)
                sg_free[j % 2] = t_yf
                hd_free[0] = t_yf
                t_ysq = G(lambda e, o=xsqb, a=yfb[:, MT * j:MT * j + MT]:
                          e.tensor_tensor(out=o, in0=a, in1=a, op=OP.mult),
                          waits=(t_yf, rs_prev[0]), inc=True)
                t_rs = T(lambda e, r=xsqb: e.matmul(
                    pb[5][0:1, 0:MT], onesb_c1, r, start=True, stop=True),
                    waits=(t_ysq, psum_free[5]), inc=True)
                rs_prev[0] = t_rs
                tok_rmsacc[0] = V(
                    lambda e, o=rowreg3[0:1, :], a=pb[5][0:1, 0:MT]:
                    e.tensor_tensor(out=o, in0=o, in1=a, op=OP.add),
                    waits=(t_rs, tok_rmsacc[0]), inc=True)
                psum_free[5] = tok_rmsacc[0]

            em_dma(0)
            em_gate(0)
            em_gsilu(0)
            em_hid(0)
            em_conv(0)
            em_csilu(0)
            em_tr(0)
            if NSL_I > 1:
                em_dma(1)
            for j in range(NSL_I):
                nx = j + 1
                em_scan_pre(j, 0)
                if nx < NSL_I:
                    em_gate(nx)
                em_scan_it(j, 0, 0)
                em_scan_it(j, 0, 1)
                em_scan_pre(j, 1)
                if nx < NSL_I:
                    em_gsilu(nx)
                    if not halo:
                        em_hid(nx)
                em_scan_it(j, 1, 0)
                if nx < NSL_I:
                    if halo:
                        em_hid(nx)
                    em_conv(nx)
                em_scan_it(j, 1, 1)
                if nx < NSL_I:
                    em_csilu(nx)
                    em_tr(nx)
                    if nx + 1 < NSL_I:
                        em_dma(nx + 1)
                em_epi(j)
            t_rmsum = tok_rmsacc[0]
            t_v1 = V(lambda e, o=rowreg3[0:1, :]:
              e.tensor_scalar(out=o, in0=o, scalar1=1.0 / INTER, scalar2=EPS,
                              op0=OP.mult, op1=OP.add), waits=(t_rmsum,),
              inc=True)
            t_s1 = Sc(lambda e, o=rowreg3[0:1, :]: e.activation(
                out=o, in_=o, func=AF.Sqrt), waits=(t_v1,), inc=True)
            t_r1 = V(lambda e, o=rowreg3[0:1, :]: e.reciprocal(out=o, in_=o),
                     waits=(t_s1,), inc=True)
            t_mm = T(lambda e: e.matmul(pb[6][:, 0:MT], onesr(0),
                                        rowreg3[0:1, :], start=True,
                                        stop=True),
                     waits=(t_r1, psum_free[6]), inc=True)
            t_rmsB = V(lambda e, o=bcA, a=pb[6][:, 0:MT]: e.tensor_copy(
                out=o, in_=a), waits=(t_mm,), inc=True)
            psum_free[6] = t_rmsB
            tok_yn = None
            for k in range(NSL_I):
                tyn1 = G(lambda e, a=yfb[:, MT * k:MT * k + MT], o=scr1, b=bcA:
                         e.tensor_tensor(out=o, in0=a, in1=b, op=OP.mult),
                         waits=(t_rmsB, tok_yn), inc=True)
                tok_yn = V(lambda e, o=yfb[:, MT * k:MT * k + MT], a=scr1,
                           s=nwcols[:, k:k + 1]: e.tensor_scalar(
                               out=o, in0=a, scalar1=s, scalar2=None,
                               op0=OP.mult), waits=(tyn1,), inc=True)
            tok_x2 = []
            for ds in range(16):
                par = ds % 2
                t_w = p.dma("gpsimd", wstr[par], w_out_img[ds],
                            waits=(w_free[par],), sem=f"s_w{par}")
                bank = 6 + (ds % 2)
                t_wo = None
                for ic in range(NSL_I):
                    t_wo = T(lambda e, l=wstr[par][:, 128 * ic:128 * ic + 128],
                             r=yfb[:, MT * ic:MT * ic + MT], ic=ic,
                             ob=pb[bank][:, 0:MT]: e.matmul(
                                 ob, l, r, start=(ic == 0),
                                 stop=(ic == NSL_I - 1)),
                             waits=(t_w, tok_yn, psum_free[bank]),
                             inc=(ic == NSL_I - 1))
                w_free[par] = t_wo
                t_x2 = V(lambda e, o=xbuf[:, MT * ds:MT * ds + MT],
                         a=pb[bank][:, 0:MT]: e.tensor_tensor(
                             out=o, in0=a, in1=o, op=OP.add),
                         waits=(t_wo,), inc=True)
                psum_free[bank] = t_x2
                tok_x2.append(t_x2)
            tok_x2_all = tok_x2[-1]
            t_mB2, _ = ln_stats(xbuf, qxb, sqstat, rowreg1[0:1, :],
                                rowreg2[0:1, :], tok_x2_all, False)
            toks = {}
            toks["mxA"] = V(lambda e, o=rmaxb, a=xbuf[:, 0:MT],
                            b=xbuf[:, MT:2 * MT]:
                            e.tensor_tensor(out=o, in0=a, in1=b, op=OP.max),
                            waits=(t_mB2,), inc=True)
            toks["mnA"] = V(lambda e, o=rmnb, a=xbuf[:, 0:MT],
                            b=xbuf[:, MT:2 * MT]:
                            e.tensor_tensor(out=o, in0=a, in1=b, op=OP.min),
                            inc=True)
            toks["mxB"] = V(lambda e, o=scr1, a=xbuf[:, 2 * MT:3 * MT],
                            b=xbuf[:, 3 * MT:4 * MT]:
                            e.tensor_tensor(out=o, in0=a, in1=b, op=OP.max),
                            inc=True)
            toks["mnB"] = V(lambda e, o=scr2, a=xbuf[:, 2 * MT:3 * MT],
                            b=xbuf[:, 3 * MT:4 * MT]:
                            e.tensor_tensor(out=o, in0=a, in1=b, op=OP.min),
                            inc=True)
            for k in range(4, NSL_E):
                kx = "mxA" if (k % 2 == 0) else "mxB"
                kn = "mnA" if (k % 2 == 0) else "mnB"
                dst_mx = rmaxb if (k % 2 == 0) else scr1
                dst_mn = rmnb if (k % 2 == 0) else scr2
                toks[kx] = V(lambda e, o=dst_mx, a=xbuf[:, MT * k:MT * k + MT]:
                             e.tensor_tensor(out=o, in0=a, in1=o, op=OP.max),
                             waits=(toks[kx],), inc=True)
                toks[kn] = V(lambda e, o=dst_mn, a=xbuf[:, MT * k:MT * k + MT]:
                             e.tensor_tensor(out=o, in0=a, in1=o, op=OP.min),
                             waits=(toks[kn],), inc=True)
            t_ac2 = V(lambda e, o=rmaxb, a=scr1: e.tensor_tensor(
                out=o, in0=o, in1=a, op=OP.max),
                waits=(toks["mxA"], toks["mxB"]), inc=True)
            t_ac3 = V(lambda e, o=rmnb, a=scr2: e.tensor_tensor(
                out=o, in0=o, in1=a, op=OP.min),
                waits=(toks["mnA"], toks["mnB"]), inc=True)
            t_ac4 = V(lambda e, o=rmaxb, a=meanB: e.tensor_tensor(
                out=o, in0=o, in1=a, op=OP.subtract), waits=(t_ac2, t_ac3),
                inc=True)
            t_ac5 = V(lambda e, o=rmnb, a=meanB: e.tensor_tensor(
                out=o, in0=a, in1=o, op=OP.subtract), waits=(t_ac3,),
                inc=True)
            t_rmx = V(lambda e, o=rmaxb, a=rmnb: e.tensor_tensor(
                out=o, in0=o, in1=a, op=OP.max), waits=(t_ac4, t_ac5),
                inc=True)

            def rowmax_to_scales(tok_ready, dst_s448, dst_scale, scale_extra_row):
                t_tp = None
                for bb in range(4):
                    t_tp = T(lambda e, a=rmaxb[:, 128 * bb:128 * bb + 128],
                             o=pb[0][:, 128 * bb:128 * bb + 128]:
                             e.transpose(o, a, identf),
                             waits=(tok_ready, psum_free[0]), inc=(bb == 3))
                trds = []
                for bb in range(4):
                    trds.append(V(lambda e, o=scr2[:, bb:bb + 1],
                      a=pb[0][:, 128 * bb:128 * bb + 128]: e.tensor_reduce(
                          out=o, in_=a, axis=AX.X, op=OP.max), waits=(t_tp,),
                      inc=True))
                t_rd2 = vtok(waits=tuple(trds))
                psum_free[0] = t_rd2
                t_tp2 = T(lambda e: e.transpose(pb[0][0:4, 0:128],
                                                scr2[:, 0:4], identf),
                          waits=(t_rd2, psum_free[0]), inc=True)
                t_ev3 = V(lambda e: e.tensor_copy(out=scr2[0:4, 0:128],
                                                  in_=pb[0][0:4, 0:128]),
                          waits=(t_tp2,), inc=True)
                psum_free[0] = t_ev3
                t_dmo = p.dma("gpsimd", row_scr[0:4, :], scr2[0:4, 0:128],
                              waits=(t_ev3, p.tok("s_rs")), sem="s_rs")
                t_dmi = p.dma("gpsimd", rowreg3[0:1, :],
                              row_scr[0:4, :].rearrange("a b -> (a b)").unsqueeze(0),
                              waits=(t_dmo,), sem="s_rs")
                trr = V(lambda e, o=rowreg4[0:1, :], a=rowreg3[0:1, :]:
                  e.reciprocal(out=o, in_=a), waits=(t_dmi,), inc=True)
                t_s448r = V(lambda e, o=rowreg4[0:1, :]: e.tensor_scalar(
                    out=o, in0=o, scalar1=448.0, scalar2=None, op0=OP.mult),
                    waits=(trr,), inc=True)
                t_m448r = V(lambda e, o=rowreg3[0:1, :]: e.tensor_scalar(
                    out=o, in0=o, scalar1=1.0 / 448.0, scalar2=None,
                    op0=OP.mult), waits=(t_dmi,), inc=True)
                if scale_extra_row is not None:
                    t_m448r = V(lambda e, o=rowreg3[0:1, :], a=scale_extra_row:
                                e.tensor_tensor(out=o, in0=o, in1=a,
                                                op=OP.mult),
                                waits=(t_m448r,), inc=True)
                t_sclr = t_m448r
                t_k1 = T(lambda e: e.matmul(pb[0][:, 0:MT], onesr(0),
                                            rowreg4[0:1, :], start=True,
                                            stop=True),
                         waits=(t_s448r, psum_free[0]), inc=True)
                t_b1_ = V(lambda e, o=dst_s448: e.tensor_copy(
                    out=o, in_=pb[0][:, 0:MT]), waits=(t_k1,), inc=True)
                psum_free[0] = t_b1_
                t_k2 = T(lambda e: e.matmul(pb[0][:, 0:MT], onesr(0),
                                            rowreg3[0:1, :], start=True,
                                            stop=True),
                         waits=(t_sclr, psum_free[0]), inc=True)
                t_b2_ = V(lambda e, o=dst_scale: e.tensor_copy(
                    out=o, in_=pb[0][:, 0:MT]), waits=(t_k2,), inc=True)
                psum_free[0] = t_b2_
                return t_b1_, t_b2_

            t_s448, t_m448 = rowmax_to_scales(t_rmx, bcA, bcB,
                                              rowreg2[0:1, :])
            tok_qx = None
            for k in range(NSL_E):
                tq1 = G(lambda e, a=xbuf[:, MT * k:MT * k + MT], o=scr2:
                        e.tensor_tensor(out=o, in0=a, in1=meanB,
                                        op=OP.subtract),
                        waits=(t_s448, tok_qx), inc=True)
                tq2 = G(lambda e, o=scr1, a=scr2, b=bcA: e.tensor_tensor(
                    out=o, in0=a, in1=b, op=OP.mult), waits=(tq1,), inc=True)
                tok_qx = emit_quant(scr1, qxb[:, MT * k:MT * k + MT], MT,
                                    vwait=tq2)
            gtoks = {}
            prev_g3 = None
            prev_t = None
            V(lambda e: e.memset(rmaxb, 0.0))
            V(lambda e: e.memset(rmnb, 0.0))
            V(lambda e: e.memset(bcA2, 0.0))
            t_rm0b = V(lambda e: e.memset(bcB2, 0.0), inc=True)
            for r in range(NSL_R):
                parA = r % 2
                parB = 2 + r % 2
                t_wgq = p.dma("sync", wstr[parA][:, 0:2048], qg_scr[r],
                              waits=(w_free[parA], tok_p0a, tok_p0b),
                              sem=f"s_g{parA}")
                t_wuq = p.dma("scalar", wstr[parB], qu_scr[r],
                              waits=(w_free[parB], tok_p0a, tok_p0b),
                              sem=f"s_g{parB}")
                t_gq = None
                for k in range(NSL_E):
                    t_gq = T(lambda e, l=wstr[parA][:, 128 * k:128 * k + 128],
                             r_=qxb[:, MT * k:MT * k + MT], k=k: e.matmul(
                                 pb[0][:, 0:MT], l, r_, start=(k == 0),
                                 stop=(k == NSL_E - 1)),
                             waits=(t_wgq, tok_qx, psum_free[0]),
                             inc=(k == NSL_E - 1))
                w_free[parA] = t_gq
                t_uq = None
                for k in range(NSL_E):
                    t_uq = T(lambda e, l=wstr[parB][:, 128 * k:128 * k + 128],
                             r_=qxb[:, MT * k:MT * k + MT], k=k: e.matmul(
                                 pb[1][:, 0:MT], l, r_, start=(k == 0),
                                 stop=(k == NSL_E - 1)),
                             waits=(t_wuq, psum_free[1]),
                             inc=(k == NSL_E - 1))
                w_free[parB] = t_uq
                t_g2 = V(lambda e, o=scr1, a=pb[0][:, 0:MT],
                         s=wsg_cols[:, r:r + 1], b=bcB: e.scalar_tensor_tensor(
                             out=o, in0=a, scalar=s, in1=b, op0=OP.mult,
                             op1=OP.mult),
                         waits=(t_gq, t_m448, prev_g3, prev_t), inc=True)
                psum_free[0] = t_g2
                t_g3 = Sc(lambda e, o=sgbuf, a=scr1: e.activation(
                    out=o, in_=a, func=AF.Silu), waits=(t_g2,), inc=True)
                prev_g3 = t_g3
                t_u2 = V(lambda e, o=scr2, a=pb[1][:, 0:MT],
                         s=wsu_cols[:, r:r + 1], b=bcB: e.scalar_tensor_tensor(
                             out=o, in0=a, scalar=s, in1=b, op0=OP.mult,
                             op1=OP.mult),
                         waits=(t_uq, t_m448, prev_t), inc=True)
                psum_free[1] = t_u2
                t_t = G(lambda e, o=tqt[:, MT * r:MT * r + MT], a=sgbuf,
                        b=scr2: e.tensor_tensor(out=o, in0=a, in1=b,
                                                op=OP.mult),
                        waits=(t_g3, t_u2, t_rm0b), inc=True)
                prev_t = t_t
                kx = "gmxA" if (r % 2 == 0) else "gmxB"
                kn = "gmnA" if (r % 2 == 0) else "gmnB"
                dst_mx = rmaxb if (r % 2 == 0) else bcA2
                dst_mn = rmnb if (r % 2 == 0) else bcB2
                gtoks[kx] = V(lambda e, o=dst_mx, a=tqt[:, MT * r:MT * r + MT]:
                              e.tensor_tensor(out=o, in0=a, in1=o, op=OP.max),
                              waits=(t_t, gtoks.get(kx)), inc=True)
                gtoks[kn] = V(lambda e, o=dst_mn, a=tqt[:, MT * r:MT * r + MT]:
                              e.tensor_tensor(out=o, in0=a, in1=o, op=OP.min),
                              waits=(gtoks.get(kn),), inc=True)
            tg2 = V(lambda e, o=rmaxb, a=bcA2: e.tensor_tensor(
                out=o, in0=o, in1=a, op=OP.max),
                waits=(gtoks["gmxA"], gtoks["gmxB"]), inc=True)
            tg3 = V(lambda e, o=rmnb, a=bcB2: e.tensor_tensor(
                out=o, in0=o, in1=a, op=OP.min),
                waits=(gtoks["gmnA"], gtoks["gmnB"]), inc=True)
            tg4 = V(lambda e, o=rmnb: e.tensor_scalar(
                out=o, in0=o, scalar1=-1.0, scalar2=None, op0=OP.mult),
                waits=(tg2, tg3), inc=True)
            t_tmax = V(lambda e, o=rmaxb, a=rmnb: e.tensor_tensor(
                out=o, in0=o, in1=a, op=OP.max), waits=(tg4,), inc=True)
            t_st448, t_mt448 = rowmax_to_scales(t_tmax, bcA, bcB, None)
            tok_qt = None
            for r in range(NSL_R):
                tv = G(lambda e, a=tqt[:, MT * r:MT * r + MT], o=scr1, b=bcA:
                       e.tensor_tensor(out=o, in0=a, in1=b, op=OP.mult),
                       waits=(t_st448, tok_qt), inc=True)
                tok_qt = emit_quant(scr1, tqt[:, MT * r:MT * r + MT], MT,
                                    vwait=tv)
            out_slab_free = [None, None]
            for hg in range(2):
                t_dn = None
                for ms in range(NSL_R):
                    par = ms % 2
                    t_wd = p.dma("sync", wstr[par][:, 0:1024],
                                 qd_scr[ms][:, 1024 * hg:1024 * hg + 1024],
                                 waits=(w_free[par],), sem=f"s_g{par}")
                    for hc in range(8):
                        t_dn = T(lambda e, l=wstr[par][:, 128 * hc:128 * hc + 128],
                                 r_=tqt[:, MT * ms:MT * ms + MT], hc=hc, ms=ms:
                                 e.matmul(pb[hc][:, 0:MT], l, r_,
                                          start=(ms == 0),
                                          stop=(ms == NSL_R - 1)),
                                 waits=((t_wd, tok_qt) +
                                        (tuple(psum_free.values())
                                         if ms == 0 else ())),
                                 inc=(hc == 7))
                    w_free[par] = t_dn
                for hc in range(8):
                    hs = 8 * hg + hc
                    par = hc % 2
                    osl = scr1 if par == 0 else scr2
                    tdv = V(lambda e, o=osl, a=pb[hc][:, 0:MT],
                            s=wsd_cols[:, hs:hs + 1], b=bcB:
                            e.scalar_tensor_tensor(
                                out=o, in0=a, scalar=s, in1=b, op0=OP.mult,
                                op1=OP.mult),
                            waits=(t_dn, t_mt448, out_slab_free[par]),
                            inc=True)
                    t_os = V(lambda e, o=osl, a=xbuf[:, MT * hs:MT * hs + MT]:
                             e.tensor_tensor(out=o, in0=o, in1=a, op=OP.add),
                             waits=(tdv,), inc=True)
                    psum_free[hc] = t_os
                    t_do = p.dma("gpsimd",
                                 outT[128 * hs:128 * hs + 128, MT * m:MT * m + MT],
                                 osl, waits=(t_os,),
                                 sem=("s_st0" if par == 0 else "s_st1"))
                    out_slab_free[par] = t_do
                    out_final = t_do
            tok_x_free = [p.tok("vx"), out_slab_free[0], out_slab_free[1]]

        p.emit("sync", lambda e: e.nop(), waits=(out_final,))

        with nc.Block() as block:
            def mk(ename):
                def body(eng):
                    for (rw, fn, inc, amt) in p.items[ename]:
                        for (s, v) in rw:
                            eng.wait_ge(semh[s], v)
                        inst = fn(eng)
                        if inc is not None:
                            inst.then_inc(semh[inc], amt)
                return body
            block.sync(mk("sync"))
            block.vector(mk("vector"))
            block.scalar(mk("scalar"))
            block.tensor(mk("tensor"))
            block.gpsimd(mk("gpsimd"))
    return nc


_NC_CACHE = [None]


def kernel(**inputs):
    x = np.asarray(inputs["L0_attn_input_states"], np.float32)
    W_in = np.asarray(inputs["L0_attn_W_in"], np.float32)
    conv_w = np.asarray(inputs["L0_attn_conv_w"], np.float32)
    conv_b = np.asarray(inputs["L0_attn_conv_b"], np.float32)
    dt_bias = np.asarray(inputs["L0_attn_dt_bias"], np.float32)
    A_log = np.asarray(inputs["L0_attn_A_log"], np.float32)
    D = np.asarray(inputs["L0_attn_D"], np.float32)
    norm_w = np.asarray(inputs["L0_attn_norm_w"], np.float32)
    W_out = np.asarray(inputs["L0_attn_W_out"], np.float32)
    w_gate = np.asarray(inputs["L0_ffn_w_gate"], np.float32)
    w_up = np.asarray(inputs["L0_ffn_w_up"], np.float32)
    w_down = np.asarray(inputs["L0_ffn_w_down"], np.float32)
    dn_idx = np.asarray(inputs["L0_ffn_down_reorder_index"])

    def img_lhsT(wT, nsl_out, kchunks):
        K, O = wT.shape
        a = wT.reshape(kchunks, 128, nsl_out, 128)
        return np.ascontiguousarray(a.transpose(2, 1, 0, 3).reshape(nsl_out, 128, kchunks * 128))

    W_inT = np.zeros((E, 67 * 128), np.float32)
    W_inT[:, 0:8512] = W_in.T
    w_in_img = img_lhsT(W_inT, 67, 16)
    w_out_img = img_lhsT(np.ascontiguousarray(W_out.T), 16, 32)
    wg_img = img_lhsT(np.ascontiguousarray(w_gate.T), NSL_R, 16)
    wu_img = img_lhsT(np.ascontiguousarray(w_up.T), NSL_R, 16)
    dinv = np.argsort(dn_idx)
    w2 = np.ascontiguousarray(w_down[:, dinv])
    wd_img = img_lhsT(np.ascontiguousarray(w2.T), 16, 44)
    wdT = np.ascontiguousarray(w2.T)
    wd_img = np.ascontiguousarray(wdT.reshape(44, 128, 16, 128).reshape(44, 128, 2048))
    wd_nat = np.ascontiguousarray(w_down.reshape(16, 128, FFI))

    conv_w_i = np.ascontiguousarray(
        conv_w.reshape(NSL_C, 128, 4).transpose(1, 0, 2).reshape(128, NSL_C * 4))
    conv_b_i = np.ascontiguousarray(conv_b.reshape(NSL_C, 128).T)
    dt_bias_i = np.ascontiguousarray(dt_bias.reshape(64, 1))
    a_log_i = np.ascontiguousarray(A_log.reshape(64, 1))
    d_cols_i = np.ascontiguousarray(np.repeat(D, 64).reshape(32, 128).T)
    nw_cols_i = np.ascontiguousarray(norm_w.reshape(32, 128).T)
    tri_i = (np.arange(128)[None, :] >= np.arange(128)[:, None]).astype(np.float32)
    ident_i = np.eye(128, dtype=np.float32)

    shared = dict(w_in_img=w_in_img, w_out_img=w_out_img, wg_img=wg_img,
                  wu_img=wu_img, wd_img=wd_img, wd_nat=wd_nat,
                  conv_w_i=conv_w_i, conv_b_i=conv_b_i, dt_bias_i=dt_bias_i,
                  a_log_i=a_log_i, d_cols_i=d_cols_i, nw_cols_i=nw_cols_i,
                  tri_i=tri_i, ident_i=ident_i)

    in_maps = []
    for core in range(8):
        b = core // 2
        h = core % 2
        t0 = h * TC
        xc = x[b, t0:t0 + TC]
        halo = x[b, t0 - 3:t0] if h == 1 else np.zeros((3, E), np.float32)
        xTc = np.ascontiguousarray(
            np.concatenate([halo, xc], 0).T)
        im = {"xT": xTc}
        im.update(shared)
        in_maps.append(im)

    if _NC_CACHE[0] is None:
        nc = build_nc()
        mybir.codegen_inst_isa_subclasses(nc)
        _NC_CACHE[0] = nc
    nc = _NC_CACHE[0]
    res = run_bass_kernel_spmd(nc, in_maps, core_ids=list(range(8)))
    out = np.empty((4, 4096, E), np.float32)
    for core in range(8):
        b = core // 2
        h = core % 2
        t0 = h * TC
        out[b, t0:t0 + TC, :] = res.results[core]["outT"].T
    return out




# version 12 (final)
# speedup vs baseline: 1.0938x; 1.0938x over previous
import contextlib
import numpy as np

import concourse.bass as bass
import concourse.mybir as mybir
from concourse.bass_utils import run_bass_kernel_spmd
from concourse import dve_ops
from concourse.dve_op import DveOp
from concourse.dve_spec import Spec, Src0, Src1, C0, lower as dve_lower

F32 = mybir.dt.float32
BF16 = mybir.dt.bfloat16
I32 = mybir.dt.int32
AF = mybir.ActivationFunctionType
OP = mybir.AluOpType
AX = mybir.AxisListType


def _register(name, spec_body, ref, victim):
    ex = next((o for o in dve_ops.OPS if o.name == name), None)
    if ex is not None:
        return ex
    spec = Spec(body=spec_body)
    shas = {v: dve_lower(spec, ver=v, perf_en=False, name=name).sha(v)
            for v in ("v3", "v4")}
    op = DveOp(name, spec, subdim=False, uops_sha=shas, reference=ref)
    vic = next(o for o in dve_ops.OPS if o.name == victim)
    idx = dve_ops.OPS.index(vic)
    row = dve_ops._SUB_OPCODE_FOR_NAME[vic.name]
    dve_ops.OPS[idx] = op
    del dve_ops._SUB_OPCODE_FOR_NAME[vic.name]
    dve_ops._SUB_OPCODE_FOR_NAME[name] = row
    dve_ops.CUSTOM_DVE_SPECS.pop(vic.name, None)
    dve_ops.CUSTOM_DVE_SPECS[name] = spec
    dve_ops.CUSTOM_DVE_REFS.pop(vic.name, None)
    dve_ops.CUSTOM_DVE_REFS[name] = op.ref
    return op


MAGIC = 12582912.0
QUANT_FIN = _register(
    "QUANT_FIN", ((Src0 + C0) - C0) * Src1,
    lambda in0, in1, s0, s1, imm2: np.rint(in0) * in1, "DVEHASH32L_BODY")

E = 2048
TC = 2048
MT = 512
NMT = TC // MT
NH = 64
INTER = 4096
CONV = 4352
NSL_E = 16
NSL_I = 32
NSL_C = 34
FFI = 5632
NSL_R = 44
EPS = 1e-5

ENGS = ("sync", "vector", "scalar", "tensor", "gpsimd")
ENG_SEM = {"vector": "vx", "scalar": "sx", "tensor": "px", "gpsimd": "gx"}


class Prog:
    def __init__(self):
        self.items = {e: [] for e in ENGS}
        self.counts = {}
        self.last_wait = {}

    def sem(self, name):
        self.counts[name] = 0

    def tok(self, sem):
        return (sem, self.counts[sem])

    def emit(self, eng, fn, waits=(), inc=None, amt=1):
        rw = []
        flat = []
        for t in waits:
            if t is None:
                continue
            if isinstance(t, list):
                flat.extend(x for x in t if x is not None)
            else:
                flat.append(t)
        for t in flat:
            s, v = t
            if v > self.last_wait.get((eng, s), 0):
                rw.append((s, v))
                self.last_wait[(eng, s)] = v
        tok = None
        if inc is not None:
            self.counts[inc] += amt
            tok = (inc, self.counts[inc])
        self.items[eng].append((rw, fn, inc, amt))
        return tok

    def dma(self, eng, out, in_, waits=(), sem=None):
        assert sem is not None
        return self.emit(eng, lambda e, o=out, i=in_: e.dma_start(out=o, in_=i),
                         waits, inc=sem, amt=16)


DBG_OFFS = {}


class Arena:
    def __init__(self, ap, total, tag=""):
        self.ap = ap
        self.total = total
        self.off = 0
        self.tag = tag

    def alloc(self, cols, p0=0, p=128, name=None):
        assert self.off + cols <= self.total, f"arena overflow {self.off}+{cols}"
        r = self.ap[p0:p0 + p, self.off:self.off + cols]
        DBG_OFFS[(self.tag, self.off)] = cols
        self.off += cols
        return r


def build_nc():
    nc = bass.Bass("TRN2", target_bir_lowering=False, debug=False)

    xT = nc.dram_tensor("xT", [E, 3 + TC], F32, kind="ExternalInput").ap()
    w_in_img = nc.dram_tensor("w_in_img", [67, 128, 2048], F32, kind="ExternalInput").ap()
    w_out_img = nc.dram_tensor("w_out_img", [16, 128, 4096], F32, kind="ExternalInput").ap()
    wg_img = nc.dram_tensor("wg_img", [NSL_R, 128, 2048], F32, kind="ExternalInput").ap()
    wu_img = nc.dram_tensor("wu_img", [NSL_R, 128, 2048], F32, kind="ExternalInput").ap()
    wd_img = nc.dram_tensor("wd_img", [NSL_R, 128, 2048], F32, kind="ExternalInput").ap()
    wd_nat = nc.dram_tensor("wd_nat", [16, 128, FFI], F32, kind="ExternalInput").ap()
    conv_w_i = nc.dram_tensor("conv_w_i", [128, NSL_C * 4], F32, kind="ExternalInput").ap()
    conv_b_i = nc.dram_tensor("conv_b_i", [128, NSL_C], F32, kind="ExternalInput").ap()
    dt_bias_i = nc.dram_tensor("dt_bias_i", [64, 1], F32, kind="ExternalInput").ap()
    a_log_i = nc.dram_tensor("a_log_i", [64, 1], F32, kind="ExternalInput").ap()
    d_cols_i = nc.dram_tensor("d_cols_i", [128, 32], F32, kind="ExternalInput").ap()
    nw_cols_i = nc.dram_tensor("nw_cols_i", [128, 32], F32, kind="ExternalInput").ap()
    tri_i = nc.dram_tensor("tri_i", [128, 128], F32, kind="ExternalInput").ap()
    ident_i = nc.dram_tensor("ident_i", [128, 128], F32, kind="ExternalInput").ap()
    outT = nc.dram_tensor("outT", [E, TC], F32, kind="ExternalOutput").ap()

    qg_scr = nc.dram_tensor("qg_scr", [NSL_R, 128, 2048], BF16, kind="Internal").ap()
    qu_scr = nc.dram_tensor("qu_scr", [NSL_R, 128, 2048], BF16, kind="Internal").ap()
    qd_scr = nc.dram_tensor("qd_scr", [NSL_R, 128, 2048], BF16, kind="Internal").ap()
    aflat = nc.dram_tensor("aflat", [64, MT], F32, kind="Internal").ap()
    row_scr = nc.dram_tensor("row_scr", [16, 128], F32, kind="Internal").ap()

    es = contextlib.ExitStack()
    with es:
        AF32 = es.enter_context(nc.sbuf_tensor("AF32", [128, 26400], F32)).ap()
        AB16 = es.enter_context(nc.sbuf_tensor("AB16", [128, 49700], BF16)).ap()
        AI32 = es.enter_context(nc.sbuf_tensor("AI32", [128, 1536], I32)).ap()
        f32 = Arena(AF32, 26400, "f")
        b16 = Arena(AB16, 49700, "b")
        i32a = Arena(AI32, 1536)

        xbuf = f32.alloc(NSL_E * MT)
        stash = f32.alloc(NSL_C * 3)
        convw = f32.alloc(NSL_C * 4)
        convb = f32.alloc(NSL_C)
        dcols = f32.alloc(32)
        nwcols = f32.alloc(32)
        tri = f32.alloc(128)
        identf = f32.alloc(128)
        dtb_c = f32.alloc(1)
        alog_c = f32.alloc(1)
        ones_c1 = f32.alloc(1)
        ones_t = f32.alloc(128)
        wsg_cols = f32.alloc(NSL_R)
        wsu_cols = f32.alloc(NSL_R)
        wsd_cols = f32.alloc(16)
        b448d = f32.alloc(2048)
        rowreg1 = f32.alloc(MT)
        rowreg2 = f32.alloc(MT)
        rowreg3 = f32.alloc(MT)
        rowreg4 = f32.alloc(MT)
        rowh = f32.alloc(8)
        cvbuf = f32.alloc(515)
        cbuf = f32.alloc(512)
        sgbuf = f32.alloc(512)
        hidbuf = f32.alloc(512)
        hdbuf = f32.alloc(512)
        mdbuf = f32.alloc(384)
        dtar = f32.alloc(512)
        dtar2 = f32.alloc(512)
        dtar3 = f32.alloc(512)
        acum_fm = f32.alloc(512)
        acumT = f32.alloc(256)
        ddtT = f32.alloc(256)
        dtT_tok = f32.alloc(256)
        b448 = f32.alloc(128)
        meanB = f32.alloc(512)
        rstdB = f32.alloc(512)
        bcA = f32.alloc(512)
        bcB = f32.alloc(512)
        scr1 = f32.alloc(512)
        scr2 = f32.alloc(512)
        rmaxb = f32.alloc(512)
        rmnb = f32.alloc(512)
        zqbuf = f32.alloc(512)
        bcA2 = f32.alloc(512)
        bcB2 = f32.alloc(512)
        xhalo_f = f32.alloc(48)
        dumb = f32.alloc(4)
        absb = f32.alloc(512)
        p0w = f32.alloc(512)
        mdbuf_p = [mdbuf, zqbuf[:, 0:384]]
        sgbuf_p = [sgbuf, bcA2]
        hidbuf_p = [hidbuf, rmaxb]
        arow = [rowreg2[64:65, :], rowreg3[64:65, :]]
        assert f32.off <= 26400, f32.off

        xnb = b16.alloc(NSL_E * MT)
        yfb = b16.alloc(NSL_I * MT)
        tqt = AB16[:, 0:NSL_R * MT]
        qxb = b16.alloc(NSL_E * MT)
        wstr = [b16.alloc(4096), b16.alloc(4096), b16.alloc(2048), b16.alloc(2048)]
        onesb_c1 = b16.alloc(1)
        hidTb = b16.alloc(512)
        btokb = b16.alloc(512)
        bbT = b16.alloc(512)
        cbT = b16.alloc(512)
        gmm = b16.alloc(768)
        mt0b = b16.alloc(256)
        mt1b = b16.alloc(128)
        eeb = b16.alloc(256)
        ceb = b16.alloc(256)
        stsb = b16.alloc(64)
        xnhalo = b16.alloc(NSL_E * 3)
        xsqb = b16.alloc(512)
        bdb = b16.alloc(256)
        mt0b_p = [mt0b, qxb[:, 0:256]]
        mt1b_p = [mt1b, qxb[:, 256:384]]
        eeb_p = [eeb, qxb[:, 384:640]]
        ceb_p = [ceb, qxb[:, 640:896]]
        stsb_p = [stsb, qxb[:, 896:960]]
        bdb_p = [bdb, qxb[:, 960:1216]]
        hidTb_p = [hidTb, qxb[:, 1216:1728]]
        scrb = wstr[0][:, 0:2048]
        sqstat = tqt[:, 0:NSL_E * MT]
        assert b16.off <= 49700, b16.off

        it0 = i32a.alloc(512)
        it1 = i32a.alloc(512)
        it2 = i32a.alloc(512)

        pb = [es.enter_context(nc.psum_tensor(f"pb{k}", [128, 512], F32)).ap()
              for k in range(8)]

        p = Prog()
        SEMS = ["vx", "sx", "px", "gx", "s_cst", "s_x", "s_xh",
                "s_w0", "s_w1", "s_w2", "s_w3", "s_q0", "s_q1",
                "s_ps0", "s_ps1", "s_st0", "s_st1", "s_ar0", "s_ar1",
                "s_af", "s_rs", "s_g0", "s_g1", "s_g2", "s_g3"]
        for s in SEMS:
            p.sem(s)
        semh = {s: es.enter_context(nc.semaphore(s)) for s in SEMS}

        last_dumb = [None]

        def V(fn, waits=(), inc=False):
            return p.emit("vector", fn, waits, inc=("vx" if inc else None))

        def vtok(waits=()):
            t = p.emit("vector",
                       lambda e: e.tensor_copy(out=dumb[0:1, 0:1],
                                               in_=dumb[0:1, 0:1]),
                       tuple(waits) + (last_dumb[0],), inc="vx")
            last_dumb[0] = t
            return t

        def Sc(fn, waits=(), inc=False):
            return p.emit("scalar", fn, waits, inc=("sx" if inc else None))

        def T(fn, waits=(), inc=False):
            return p.emit("tensor", fn, waits, inc=("px" if inc else None))

        def G(fn, waits=(), inc=False):
            return p.emit("gpsimd", fn, waits, inc=("gx" if inc else None))

        def fv(ap):
            return ap.bitcast(F32)

        def onesr(base):
            return ones_t[base:base + 1, :]

        psum_free = {k: None for k in range(8)}
        w_free = {0: None, 1: None, 2: None, 3: None}

        def emit_quant(v_ap, out_ap, w, inc=True, vwait=None, vwait2=None):
            bits = v_ap.bitcast(I32)
            ia, ib, ic = it0[:, 0:w], it1[:, 0:w], it2[:, 0:w]
            t0_ = V(lambda e, a=bits, o=ia: e.tensor_scalar(
                out=o, in0=a, scalar1=0x7F800000, scalar2=None,
                op0=OP.bitwise_and), waits=(vwait, vwait2), inc=True)
            t1_ = V(lambda e, a=ia, o=ib: e.tensor_scalar(
                out=o, in0=a, scalar1=121 << 23, scalar2=3 << 23,
                op0=OP.max, op1=OP.subtract), waits=(t0_,), inc=True)
            t1b_ = V(lambda e, o=ic, b=ib: e.tensor_scalar(
                out=o, in0=b, scalar1=254 << 23, scalar2=-1,
                op0=OP.subtract, op1=OP.mult), waits=(t1_,), inc=True)
            t2_ = V(lambda e, a=v_ap, b=fv(ic), o=zqbuf[:, 0:w]:
                    e.tensor_tensor(out=o, in0=a, in1=b, op=OP.mult),
                    waits=(t1b_,), inc=True)
            return V(lambda e, a=zqbuf[:, 0:w], b=fv(ib), o=out_ap:
                     e._custom_dve(QUANT_FIN, out=o, in0=a, in1=b, s0=MAGIC,
                                   s1=0.0, imm2=0.0),
                     waits=(t2_,), inc=True)

        p.dma("gpsimd", convw, conv_w_i, sem="s_cst")
        p.dma("gpsimd", convb, conv_b_i, sem="s_cst")
        p.dma("gpsimd", dcols, d_cols_i, sem="s_cst")
        p.dma("gpsimd", nwcols, nw_cols_i, sem="s_cst")
        p.dma("gpsimd", tri, tri_i, sem="s_cst")
        p.dma("gpsimd", identf, ident_i, sem="s_cst")
        p.dma("gpsimd", dtb_c[0:64, :], dt_bias_i, sem="s_cst")
        tok_consts = p.dma("gpsimd", alog_c[0:64, :], a_log_i, sem="s_cst")
        V(lambda e: e.memset(ones_c1, 1.0))
        V(lambda e: e.memset(it0, 0))
        V(lambda e: e.memset(it1, 0))
        V(lambda e: e.memset(it2, 0))
        V(lambda e: e.memset(scr1, 0.0))
        V(lambda e: e.memset(dumb, 0.0))
        tok_ms = V(lambda e: e.memset(scr2, 0.0), inc=True)
        V(lambda e: e.memset(onesb_c1, 1.0))
        V(lambda e: e.memset(stash, 0.0))
        V(lambda e: e.memset(rmaxb, 0.0))
        tok_ones = V(lambda e: e.memset(ones_t, 1.0), inc=True)

        tok_prev = None
        for hs in range(16):
            t1 = p.dma("sync", xbuf[:, 0:2816], wd_nat[hs][:, 0:2816],
                       waits=(tok_prev,), sem="s_q0")
            t2 = p.dma("sync", xbuf[:, 2816:5632], wd_nat[hs][:, 2816:5632],
                       waits=(tok_prev,), sem="s_q1")
            tr1 = V(lambda e, o=scr1[:, 0:1], a=xbuf[:, 0:2816]: e.tensor_reduce(
                out=o, in_=a, axis=AX.X, op=OP.max, apply_absolute_value=True),
              waits=(t1, tok_ms, tok_prev), inc=True)
            tr2 = V(lambda e, o=scr1[:, 1:2], a=xbuf[:, 2816:5632]: e.tensor_reduce(
                out=o, in_=a, axis=AX.X, op=OP.max, apply_absolute_value=True),
              waits=(t2,), inc=True)
            tok_prev = V(lambda e, o=wsd_cols[:, hs:hs + 1], a=scr1[:, 0:1],
                         b=scr1[:, 1:2]: e.tensor_tensor(
                out=o, in0=a, in1=b, op=OP.max), waits=(tr1, tr2), inc=True)
        trc = V(lambda e, o=scr1[:, 0:16], a=wsd_cols: e.reciprocal(
            out=o, in_=a), waits=(tok_prev,), inc=True)
        t_rd = V(lambda e, o=scr1[:, 0:16]: e.tensor_scalar(
            out=o, in0=o, scalar1=448.0, scalar2=None, op0=OP.mult),
            waits=(trc,), inc=True)
        V(lambda e, o=wsd_cols: e.tensor_scalar(
            out=o, in0=o, scalar1=1.0 / 448.0, scalar2=None, op0=OP.mult),
          waits=(trc,))
        t_tr = T(lambda e: e.transpose(pb[0][0:16, 0:128], scr1[:, 0:16], identf),
                 waits=(t_rd, tok_consts), inc=True)
        t_ev = V(lambda e: e.tensor_copy(out=scr2[0:16, 0:128],
                                         in_=pb[0][0:16, 0:128]),
                 waits=(t_tr,), inc=True)
        t_b = p.dma("gpsimd", row_scr, scr2[0:16, 0:128], waits=(t_ev,), sem="s_rs")
        t_kb = None
        for q in range(4):
            t_b2 = p.dma("gpsimd",
                         arow[0].rearrange("q (a b) -> q a b", b=128),
                         row_scr[4 * q:4 * q + 4, :].unsqueeze(0),
                         waits=(t_b, t_kb), sem="s_ar0")
            t_mm = T(lambda e, rhs=arow[0], o=pb[1]: e.matmul(
                o[:, 0:512], onesr(64), rhs, start=True, stop=True),
                waits=(t_b2, tok_ones), inc=True)
            t_kb = V(lambda e, o=b448d[:, 512 * q:512 * q + 512], a=pb[1]:
                     e.tensor_copy(out=o, in_=a[:, 0:512]), waits=(t_mm,),
                     inc=True)

        def phase0_img(img, scr, ws_cols, per_slab_scale, prev_store,
                       buf_free, carry):
            for s in range(NSL_R):
                par = s % 2
                st32 = xbuf[:, 2048 * par:2048 * par + 2048]
                t_ld = p.dma("sync", st32, img[s], waits=(buf_free[par],),
                             sem=("s_q0" if par == 0 else "s_q1"))
                if per_slab_scale:
                    accT = None
                    for ckk in range(4):
                        absX = absb if ckk % 2 == 0 else rmaxb
                        abkey = "ab%d" % (ckk % 2)
                        t_ab = Sc(lambda e, o=absX,
                                  a=st32[:, 512 * ckk:512 * ckk + 512]:
                                  e.activation(out=o, in_=a, func=AF.Abs),
                                  waits=(t_ld, carry.get(abkey)), inc=True)
                        t_f1 = V(lambda e, o=absX[:, 0:256], a=absX[:, 0:256],
                                 b=absX[:, 256:512]: e.tensor_tensor(
                                     out=o, in0=o, in1=b, op=OP.max),
                                 waits=(t_ab,), inc=True)
                        if ckk == 0:
                            accT = V(lambda e, o=scr1[:, 0:128],
                                     a=absX[:, 0:128], b=absX[:, 128:256]:
                                     e.tensor_tensor(out=o, in0=a, in1=b,
                                                     op=OP.max),
                                     waits=(t_f1, carry.get("tr")), inc=True)
                        else:
                            t_f2 = V(lambda e, o=absX[:, 0:128],
                                     a=absX[:, 0:128], b=absX[:, 128:256]:
                                     e.tensor_tensor(out=o, in0=o, in1=b,
                                                     op=OP.max),
                                     waits=(t_f1,), inc=True)
                            accT = V(lambda e, o=scr1[:, 0:128],
                                     a=absX[:, 0:128]: e.tensor_tensor(
                                         out=o, in0=o, in1=a, op=OP.max),
                                     waits=(t_f2, accT), inc=True)
                        carry[abkey] = accT
                    carry["stats"] = accT
                    t_t = accT
                    t_tr2 = T(lambda e: e.transpose(pb[0][:, 0:128],
                                                    scr1[:, 0:128], identf),
                              waits=(t_t, psum_free[0]), inc=True)
                    carry["tr"] = t_tr2
                    twr = V(lambda e, o=ws_cols[:, s:s + 1], a=pb[0][:, 0:128]:
                            e.tensor_reduce(out=o, in_=a, axis=AX.X, op=OP.max),
                            waits=(t_tr2,), inc=True)
                    twr2 = V(lambda e, o=scr2[:, 0:1], a=ws_cols[:, s:s + 1]:
                             e.reciprocal(out=o, in_=a), waits=(twr,), inc=True)
                    V(lambda e, o=ws_cols[:, s:s + 1]: e.tensor_scalar(
                        out=o, in0=o, scalar1=1.0 / 448.0, scalar2=None,
                        op0=OP.mult), waits=(twr2,))
                    t_rc = V(lambda e, o=scr2[:, 0:1]: e.tensor_scalar(
                        out=o, in0=o, scalar1=448.0, scalar2=None,
                        op0=OP.mult), waits=(twr2,), inc=True)
                    psum_free[0] = t_rc
                    t_tr3 = T(lambda e: e.transpose(pb[0][0:1, 0:128],
                                                    scr2[:, 0:1], identf),
                              waits=(t_rc, psum_free[0]), inc=True)
                    t_ev2 = V(lambda e: e.tensor_copy(
                        out=scr2[0:1, 0:128], in_=pb[0][0:1, 0:128]),
                        waits=(t_tr3,), inc=True)
                    t_mm2 = T(lambda e: e.matmul(
                        pb[1][:, 0:128], onesr(0)[:, 0:128], scr2[0:1, 0:128],
                        start=True, stop=True), waits=(t_ev2, psum_free[1]),
                        inc=True)
                    t_b448 = V(lambda e: e.tensor_copy(out=b448,
                                                       in_=pb[1][:, 0:128]),
                               waits=(t_mm2,), inc=True)
                    psum_free[1] = t_b448
                else:
                    t_b448 = t_ld
                tok_q = None
                for half in range(4):
                    p0wX = p0w if half % 2 == 0 else rmnb
                    qkey = "qf%d" % (half % 2)
                    tG = None
                    for i in range(4):
                        k = 4 * half + i
                        src_sc = (b448 if per_slab_scale
                                  else b448d[:, 128 * k:128 * k + 128])
                        tG = G(lambda e, o=p0wX[:, 128 * i:128 * i + 128],
                               a=st32[:, 128 * k:128 * k + 128], b=src_sc:
                               e.tensor_tensor(out=o, in0=a, in1=b,
                                               op=OP.mult),
                               waits=(t_ld, t_b448, carry.get(qkey),
                                      carry.get("b448d")), inc=(i == 3))
                    tok_q = emit_quant(p0wX,
                                       scrb[:, 512 * half:512 * half + 512],
                                       512, vwait=tG,
                                       vwait2=(tok_q if tok_q
                                               else prev_store[0]))
                    carry[qkey] = tok_q
                t_st = p.dma("scalar", scr[s], scrb, waits=(tok_q,),
                             sem=("s_ps0" if par == 0 else "s_ps1"))
                buf_free[par] = t_st
                prev_store[0] = t_st

        prev_store = [None]
        p0_buf_free = [t_kb, t_kb]
        p0carry = {"b448d": t_kb}
        phase0_img(wg_img, qg_scr, wsg_cols, True, prev_store, p0_buf_free,
                   p0carry)
        phase0_img(wu_img, qu_scr, wsu_cols, True, prev_store, p0_buf_free,
                   p0carry)
        phase0_img(wd_img, qd_scr, None, False, prev_store, p0_buf_free,
                   p0carry)
        tok_p0a = p.tok("s_ps0")
        tok_p0b = p.tok("s_ps1")
        tok_p0 = tok_p0a
        w_free = {0: tok_p0a, 1: tok_p0b, 2: tok_p0b, 3: tok_p0a}

        def ln_stats(src, cast_dst, sq_dst, mean_row, rstd_row, wait_tok,
                     need_rstd_b):
            t_c = V(lambda e, a=src, o=cast_dst: e.tensor_copy(out=o, in_=a),
                    waits=(wait_tok,), inc=True)
            t_sq = Sc(lambda e, a=cast_dst, o=sq_dst: e.activation(
                out=o, in_=a, func=AF.Square), waits=(t_c,), inc=True)
            for k in range(NSL_E):
                T(lambda e, r=cast_dst[:, MT * k:MT * k + MT], k=k: e.matmul(
                    pb[5][0:1, 0:MT], onesb_c1, r, start=(k == 0),
                    stop=(k == NSL_E - 1)), waits=(t_c, psum_free[5]))
            t_s2 = None
            for k in range(NSL_E):
                t_s2 = T(lambda e, r=sq_dst[:, MT * k:MT * k + MT], k=k:
                         e.matmul(pb[5][32:33, 0:MT], onesb_c1, r,
                                  start=(k == 0), stop=(k == NSL_E - 1)),
                         waits=(t_sq,), inc=(k == NSL_E - 1))
            tm_ = V(lambda e, o=mean_row, a=pb[5][0:1, 0:MT]: e.tensor_scalar(
                out=o, in0=a, scalar1=1.0 / E, scalar2=None, op0=OP.mult),
              waits=(t_s2,), inc=True)
            te_ = V(lambda e, o=rstd_row, a=pb[5][32:33, 0:MT]: e.tensor_scalar(
                out=o, in0=a, scalar1=1.0 / E, scalar2=None, op0=OP.mult),
                inc=True)
            tq_ = V(lambda e, o=scr1[0:1, 0:MT], a=mean_row: e.tensor_tensor(
                out=o, in0=a, in1=a, op=OP.mult), waits=(tm_,), inc=True)
            ts_ = V(lambda e, o=rstd_row, a=scr1[0:1, 0:MT]: e.tensor_tensor(
                out=o, in0=o, in1=a, op=OP.subtract), waits=(te_, tq_),
                inc=True)
            t_v = V(lambda e, o=rstd_row: e.tensor_scalar(
                out=o, in0=o, scalar1=EPS, scalar2=None, op0=OP.add),
                waits=(ts_,), inc=True)
            psum_free[5] = t_v
            t_sq2 = Sc(lambda e, o=rstd_row: e.activation(
                out=o, in_=o, func=AF.Sqrt), waits=(t_v,), inc=True)
            t_rp = V(lambda e, o=rstd_row: e.reciprocal(out=o, in_=o),
                     waits=(t_sq2,), inc=True)
            t_m1 = T(lambda e, r=mean_row: e.matmul(
                pb[6][:, 0:MT], onesr(0), r, start=True, stop=True),
                waits=(t_rp, psum_free[6]), inc=True)
            t_e1 = V(lambda e, o=meanB: e.tensor_copy(out=o, in_=pb[6][:, 0:MT]),
                     waits=(t_m1,), inc=True)
            psum_free[6] = t_e1
            t_e2 = t_e1
            if need_rstd_b:
                t_m2 = T(lambda e, r=rstd_row: e.matmul(
                    pb[7][:, 0:MT], onesr(0), r, start=True, stop=True),
                    waits=(t_rp, psum_free[7]), inc=True)
                t_e2 = V(lambda e, o=rstdB: e.tensor_copy(
                    out=o, in_=pb[7][:, 0:MT]), waits=(t_m2,), inc=True)
                psum_free[7] = t_e2
            return t_e1, t_e2

        def proj_slab(wb, bank, rhs_b16, halo_bank=None, halo_rhs=None,
                      wait_w=None, wait_rhs=None, wait_halo=None):
            tok = None
            for k in range(NSL_E):
                tok = T(lambda e, l=wb[:, 128 * k:128 * k + 128],
                        r=rhs_b16[:, MT * k:MT * k + MT], k=k: e.matmul(
                            pb[bank][:, 0:MT], l, r, start=(k == 0),
                            stop=(k == NSL_E - 1)),
                        waits=(wait_w, wait_rhs, psum_free[bank]),
                        inc=(k == NSL_E - 1))
            tokh = None
            if halo_bank is not None:
                for k in range(NSL_E):
                    tokh = T(lambda e, l=wb[:, 128 * k:128 * k + 128],
                             r=halo_rhs[:, 3 * k:3 * k + 3], k=k: e.matmul(
                                 pb[halo_bank][:, 0:3], l, r, start=(k == 0),
                                 stop=(k == NSL_E - 1)),
                             waits=(psum_free[halo_bank], wait_halo),
                             inc=(k == NSL_E - 1))
            return tok, tokh

        cv_free = [None]

        def conv_silu(csl, bank, out_f32, tok_mm, tok_halo, halo_from_psum,
                      halo_bank):
            if halo_from_psum:
                t_s0 = V(lambda e, o=cvbuf[:, 0:3], a=pb[halo_bank][:, 0:3]:
                         e.tensor_copy(out=o, in_=a),
                         waits=(tok_halo, cv_free[0]), inc=True)
                psum_free[halo_bank] = t_s0
            else:
                t_s0 = G(lambda e, o=cvbuf[:, 0:3],
                         a=stash[:, 3 * csl:3 * csl + 3]:
                         e.tensor_copy(out=o, in_=a), waits=(cv_free[0],),
                         inc=True)
            t_ev = Sc(lambda e, o=cvbuf[:, 3:515], a=pb[bank][:, 0:MT]:
                      e.activation(out=o, in_=a, func=AF.Copy),
                      waits=(tok_mm, cv_free[0]), inc=True)
            psum_free[bank] = t_ev
            t_us = G(lambda e, o=stash[:, 3 * csl:3 * csl + 3],
                     a=cvbuf[:, 512:515]: e.tensor_copy(out=o, in_=a),
                     waits=(t_ev, t_s0), inc=True)
            V(lambda e, o=cbuf, a=cvbuf[:, 0:512],
              w0=convw[:, 4 * csl:4 * csl + 1], b=convb[:, csl:csl + 1]:
              e.tensor_scalar(out=o, in0=a, scalar1=w0, scalar2=b,
                              op0=OP.mult, op1=OP.add), waits=(t_ev, t_s0))
            V(lambda e, o=hdbuf, a=cvbuf[:, 1:513],
              w=convw[:, 4 * csl + 1:4 * csl + 2]:
              e.tensor_scalar(out=o, in0=a, scalar1=w, scalar2=None,
                              op0=OP.mult))
            V(lambda e, o=scr1, a=cvbuf[:, 2:514],
              w=convw[:, 4 * csl + 2:4 * csl + 3]:
              e.tensor_scalar(out=o, in0=a, scalar1=w, scalar2=None,
                              op0=OP.mult))
            t_d4 = V(lambda e, o=scr2, a=cvbuf[:, 3:515],
                     w=convw[:, 4 * csl + 3:4 * csl + 4]:
                     e.tensor_scalar(out=o, in0=a, scalar1=w, scalar2=None,
                                     op0=OP.mult), inc=True)
            t_ab_ = V(lambda e, o=cbuf, a=hdbuf: e.tensor_tensor(
                out=o, in0=o, in1=a, op=OP.add), waits=(t_d4,), inc=True)
            t_cd_ = V(lambda e, o=scr1, a=scr2: e.tensor_tensor(
                out=o, in0=o, in1=a, op=OP.add), inc=True)
            t_cv = V(lambda e, o=cbuf, a=scr1: e.tensor_tensor(
                out=o, in0=o, in1=a, op=OP.add), waits=(t_ab_, t_cd_),
                inc=True)
            cv_free[0] = t_cv
            t_si = Sc(lambda e, o=out_f32, a=cbuf: e.activation(
                out=o, in_=a, func=AF.Silu), waits=(t_cv,), inc=True)
            return t_si

        out_final = None
        tok_x_free = tok_p0
        for m in range(NMT):
            col0 = 3 + MT * m
            t_x = p.dma("gpsimd",
                        xbuf.rearrange("p (s t) -> p s t", t=MT),
                        xT.rearrange("(s p) t -> p s t", p=128)[:, :, col0:col0 + MT],
                        waits=(tok_x_free, tok_p0a, tok_p0b), sem="s_x")
            if m == 0:
                t_xh = p.dma("gpsimd",
                             xhalo_f.rearrange("p (s t) -> p s t", t=3),
                             xT.rearrange("(s p) t -> p s t", p=128)[:, :, 0:3],
                             sem="s_xh")
            t_mB, t_rB = ln_stats(xbuf, qxb, sqstat, rowreg1[0:1, :],
                                  rowreg2[0:1, :], t_x, True)
            tok_xn = None
            for k in range(NSL_E):
                tsub = G(lambda e, a=xbuf[:, MT * k:MT * k + MT], o=bcB:
                         e.tensor_tensor(out=o, in0=a, in1=meanB,
                                         op=OP.subtract),
                         waits=(t_mB, t_rB, tok_xn), inc=True)
                tok_xn = G(lambda e, o=xnb[:, MT * k:MT * k + MT], a=bcB:
                           e.tensor_tensor(out=o, in0=a, in1=rstdB,
                                           op=OP.mult), waits=(tsub,),
                           inc=True)
            if m == 0:
                t_ch = V(lambda e, a=xhalo_f, o=xnhalo: e.tensor_copy(
                    out=o, in_=a), waits=(t_xh,), inc=True)
                t_sqh = Sc(lambda e, a=xnhalo, o=xsqb[:, 0:48]: e.activation(
                    out=o, in_=a, func=AF.Square), waits=(t_ch,), inc=True)
                for k in range(NSL_E):
                    T(lambda e, r=xnhalo[:, 3 * k:3 * k + 3], k=k: e.matmul(
                        pb[5][64:65, 0:3], onesb_c1, r, start=(k == 0),
                        stop=(k == NSL_E - 1)), waits=(t_ch, psum_free[5]))
                t_sh = None
                for k in range(NSL_E):
                    t_sh = T(lambda e, r=xsqb[:, 3 * k:3 * k + 3], k=k:
                             e.matmul(pb[5][64:65, 8:11], onesb_c1, r,
                                      start=(k == 0), stop=(k == NSL_E - 1)),
                             waits=(t_sqh,), inc=(k == NSL_E - 1))
                th1 = V(lambda e, o=rowh[0:1, 0:3], a=pb[5][64:65, 0:3]:
                  e.tensor_scalar(out=o, in0=a, scalar1=1.0 / E, scalar2=None,
                                  op0=OP.mult), waits=(t_sh,), inc=True)
                th2 = V(lambda e, o=rowh[0:1, 4:7], a=pb[5][64:65, 8:11]:
                  e.tensor_scalar(out=o, in0=a, scalar1=1.0 / E, scalar2=None,
                                  op0=OP.mult), inc=True)
                th3 = V(lambda e, o=scr1[0:1, 0:3], a=rowh[0:1, 0:3]:
                  e.tensor_tensor(out=o, in0=a, in1=a, op=OP.mult),
                  waits=(th1,), inc=True)
                th4 = V(lambda e, o=rowh[0:1, 4:7], a=scr1[0:1, 0:3]:
                  e.tensor_tensor(out=o, in0=o, in1=a, op=OP.subtract),
                  waits=(th2, th3), inc=True)
                t_hv = V(lambda e, o=rowh[0:1, 4:7]: e.tensor_scalar(
                    out=o, in0=o, scalar1=EPS, scalar2=None, op0=OP.add),
                    waits=(th4,), inc=True)
                psum_free[5] = t_hv
                t_hs = Sc(lambda e, o=rowh[0:1, 4:7]: e.activation(
                    out=o, in_=o, func=AF.Sqrt), waits=(t_hv,), inc=True)
                t_hr = V(lambda e, o=rowh[0:1, 4:7]: e.reciprocal(
                    out=o, in_=o), waits=(t_hs,), inc=True)
                t_hm1 = T(lambda e: e.matmul(pb[0][:, 0:3], onesr(0),
                                             rowh[0:1, 0:3], start=True,
                                             stop=True),
                          waits=(t_hr, psum_free[0]), inc=True)
                t_hm2 = T(lambda e: e.matmul(pb[1][:, 0:3],
                                             onesr(0),
                                             rowh[0:1, 4:7], start=True,
                                             stop=True), waits=(psum_free[1],),
                          inc=True)
                tok_xnh = None
                for k in range(NSL_E):
                    tsub = V(lambda e, a=xhalo_f[:, 3 * k:3 * k + 3],
                             o=scr1[:, 0:3]:
                             e.tensor_tensor(out=o, in0=a, in1=pb[0][:, 0:3],
                                             op=OP.subtract),
                             waits=(t_hm1, t_hm2, tok_xnh), inc=True)
                    tok_xnh = V(lambda e, o=xnhalo[:, 3 * k:3 * k + 3],
                                a=scr1[:, 0:3]:
                                e.tensor_tensor(out=o, in0=a,
                                                in1=pb[1][:, 0:3],
                                                op=OP.mult), waits=(tsub,),
                                inc=True)
                psum_free[0] = tok_xnh
                psum_free[1] = tok_xnh
            else:
                tok_xnh = None
            tok_bbT = tok_cbT = tok_btok = tok_acumT = tok_ddtT = tok_dtT = tok_afl = None
            for jj, kind in enumerate(["B", "C", "dt"]):
                jsl = 64 + jj
                par = jsl % 2
                t_w = p.dma("gpsimd", wstr[par][:, 0:2048], w_in_img[jsl],
                            waits=(w_free[par],), sem=f"s_w{par}")
                halo = (m == 0 and kind in ("B", "C"))
                t_mm, t_h = proj_slab(wstr[par], 0, xnb,
                                      halo_bank=(3 if halo else None),
                                      halo_rhs=(xnhalo if halo else None),
                                      wait_w=t_w, wait_rhs=tok_xn,
                                      wait_halo=tok_xnh)
                w_free[par] = t_mm
                if kind == "dt":
                    t_ev = V(lambda e, o=dtar[0:64, :], a=pb[0][0:64, 0:MT]:
                             e.tensor_scalar(out=o, in0=a,
                                             scalar1=dtb_c[0:64, :],
                                             scalar2=None, op0=OP.add),
                             waits=(t_mm, tok_consts), inc=True)
                    psum_free[0] = t_ev
                    t_ab = Sc(lambda e, o=dtar2[0:64, :], a=dtar[0:64, :]:
                              e.activation(out=o, in_=a, func=AF.Abs),
                              waits=(t_ev,), inc=True)
                    t_ex = Sc(lambda e, o=dtar2[0:64, :], a=dtar2[0:64, :]:
                              e.activation(out=o, in_=a, func=AF.Exp,
                                           scale=-1.0), waits=(t_ab,),
                              inc=True)
                    t_ln = Sc(lambda e, o=dtar2[0:64, :], a=dtar2[0:64, :]:
                              e.activation(out=o, in_=a, func=AF.Ln,
                                           bias=ones_c1[0:64, :]),
                              waits=(t_ex,), inc=True)
                    t_mx = V(lambda e, o=dtar3[0:64, :], a=dtar[0:64, :]:
                      e.tensor_scalar(out=o, in0=a, scalar1=0.0, scalar2=None,
                                      op0=OP.max), waits=(t_ln,), inc=True)
                    t_dtv = V(lambda e, o=dtar3[0:64, :], a=dtar2[0:64, :]:
                      e.tensor_tensor(out=o, in0=o, in1=a, op=OP.add),
                      waits=(t_mx,), inc=True)
                    t_A = Sc(lambda e, o=scr2[0:64, 0:1], a=alog_c[0:64, :]:
                             e.activation(out=o, in_=a, func=AF.Exp),
                             waits=(tok_consts,), inc=True)
                    t_nA = V(lambda e, o=scr2[0:64, 0:1]: e.tensor_scalar(
                        out=o, in0=o, scalar1=-1.0, scalar2=None, op0=OP.mult),
                      waits=(t_A,), inc=True)
                    t_at = V(lambda e, o=dtar[0:64, :], a=dtar3[0:64, :]:
                      e.tensor_scalar(out=o, in0=a, scalar1=scr2[0:64, 0:1],
                                      scalar2=None, op0=OP.mult),
                      waits=(t_dtv, t_nA), inc=True)
                    tsc = [t_at]
                    for c in range(2):
                        tsc.append(V(lambda e, o=acum_fm[0:64, 256 * c:256 * c + 256],
                          a=dtar[0:64, 256 * c:256 * c + 256]:
                          e.tensor_tensor_scan(out=o, data0=a, data1=a,
                                               initial=0.0, op0=OP.add,
                                               op1=OP.bypass),
                          waits=(t_at,), inc=True))
                    t_ac = vtok(waits=tuple(tsc))
                    for c in range(2):
                        Sc(lambda e, o=dtar2[0:64, 256 * c:256 * c + 256],
                           a=acum_fm[0:64, 256 * c:256 * c + 256],
                           bcol=acum_fm[0:64, 256 * c + 255:256 * c + 256]:
                           e.activation(out=o, in_=a, func=AF.Exp, scale=-1.0,
                                        bias=bcol), waits=(t_ac,))
                    t_dc = Sc(lambda e: e.activation(
                        out=scr2[0:1, 1:2], in_=scr2[0:1, 1:2],
                        func=AF.Copy), inc=True)
                    t_dd = V(lambda e, o=dtar2[0:64, :], a=dtar3[0:64, :]:
                             e.tensor_tensor(out=o, in0=o, in1=a, op=OP.mult),
                             waits=(t_dc, t_dtv), inc=True)
                    for (src, dst) in [(acum_fm, acumT), (dtar2, ddtT),
                                       (dtar3, dtT_tok)]:
                        tt = None
                        for bb in range(4):
                            tt = T(lambda e, a=src[0:64, 128 * bb:128 * bb + 128],
                                   o=pb[2][0:128, 64 * bb:64 * bb + 64]:
                                   e.transpose(o, a, identf[0:64, 0:64]),
                                   waits=(t_dd, psum_free[2]),
                                   inc=(bb == 3))
                        tok_tr = V(lambda e, o=dst, a=pb[2][:, 0:256]:
                                   e.tensor_copy(out=o, in_=a), waits=(tt,),
                                   inc=True)
                        psum_free[2] = tok_tr
                        if src is acum_fm:
                            tok_acumT = tok_tr
                        elif src is dtar2:
                            tok_ddtT = tok_tr
                        else:
                            tok_dtT = tok_tr
                    tok_afl = p.dma("gpsimd", aflat, acum_fm[0:64, :],
                                    waits=(t_ac, p.tok("s_ar0"),
                                           p.tok("s_ar1")), sem="s_af")
                else:
                    csl = 32 + jj
                    t_si = conv_silu(csl, 0, hidbuf, t_mm, t_h,
                                     halo_from_psum=(m == 0), halo_bank=3)
                    dst = bbT if kind == "B" else cbT
                    tok_bc = V(lambda e, o=dst, a=hidbuf: e.tensor_copy(
                        out=o, in_=a), waits=(t_si,), inc=True)
                    if kind == "B":
                        tok_bbT = tok_bc
                        t_bt = None
                        for bb in range(4):
                            t_bt = T(lambda e, a=hidbuf[:, 128 * bb:128 * bb + 128],
                                     o=pb[2][:, 128 * bb:128 * bb + 128]:
                                     e.transpose(o, a, identf),
                                     waits=(t_si, psum_free[2]),
                                     inc=(bb == 3))
                        tok_btok = V(lambda e, o=btokb, a=pb[2][:, 0:512]:
                                     e.tensor_copy(out=o, in_=a),
                                     waits=(t_bt,), inc=True)
                        psum_free[2] = tok_btok
                    else:
                        tok_cbT = tok_bc
            for c in range(2):
                for sb in range(2):
                    lw = 256 if sb == 0 else 128
                    l0 = 256 * c + 128 * sb
                    t_g = T(lambda e, l=bbT[:, 256 * c + 128 * sb:256 * c + 128 * sb + 128],
                            r=cbT[:, l0:l0 + lw], lw=lw: e.matmul(
                                pb[3][:, 0:lw], l, r, start=True, stop=True),
                            waits=(tok_bbT, tok_cbT, psum_free[3]), inc=True)
                    off = 384 * c + 256 * sb
                    V(lambda e, o=gmm[:, off:off + 128], a=pb[3][:, 0:128]:
                      e.tensor_tensor(out=o, in0=a, in1=tri, op=OP.mult),
                      waits=(t_g, tok_consts))
                    if sb == 0:
                        V(lambda e, o=gmm[:, off + 128:off + 256],
                          a=pb[3][:, 128:256]: e.tensor_copy(out=o, in_=a))
                    psum_free[3] = vtok()
            tok_gmm = p.tok("vx")

            tok_rmsacc = [V(lambda e: e.memset(rowreg3[0:1, :], 0.0),
                            inc=True)]
            arow_free = [None, None]
            prev_iter = [None, None]
            fs = [dict() for _ in range(NSL_I)]
            sg_free = [None, None]
            hd_free = [None]
            rs_prev = [None]
            halo = (m == 0)

            def em_dma(j):
                parA, parB = j % 2, 2 + j % 2
                fs[j]["wg"] = p.dma("gpsimd", wstr[parA][:, 0:2048],
                                    w_in_img[j], waits=(w_free[parA],),
                                    sem=f"s_w{parA}")
                fs[j]["wh"] = p.dma("gpsimd", wstr[parB], w_in_img[32 + j],
                                    waits=(w_free[parB],), sem=f"s_w{parB}")

            def em_gate(j):
                parA = j % 2
                t_gm, _ = proj_slab(wstr[parA], 0, xnb, wait_w=fs[j]["wg"],
                                    wait_rhs=tok_xn)
                w_free[parA] = t_gm
                fs[j]["gm"] = t_gm

            def em_gsilu(j):
                t_sg = Sc(lambda e, o=sgbuf_p[j % 2], a=pb[0][:, 0:MT]:
                          e.activation(out=o, in_=a, func=AF.Silu),
                          waits=(fs[j]["gm"], sg_free[j % 2]), inc=True)
                psum_free[0] = t_sg
                fs[j]["sg"] = t_sg

            def em_hid(j):
                parB = 2 + j % 2
                t_hmm, t_hh = proj_slab(wstr[parB], 0, xnb,
                                        halo_bank=(3 if halo else None),
                                        halo_rhs=(xnhalo if halo else None),
                                        wait_w=fs[j]["wh"], wait_rhs=tok_xn,
                                        wait_halo=(tok_xnh if halo else None))
                w_free[parB] = t_hmm
                fs[j]["hmm"] = t_hmm
                fs[j]["hh"] = t_hh

            def em_conv(j):
                if halo:
                    t_s0 = V(lambda e, o=cvbuf[:, 0:3], a=pb[3][:, 0:3]:
                             e.tensor_copy(out=o, in_=a),
                             waits=(fs[j]["hh"], cv_free[0]), inc=True)
                    psum_free[3] = t_s0
                else:
                    t_s0 = G(lambda e, o=cvbuf[:, 0:3],
                             a=stash[:, 3 * j:3 * j + 3]:
                             e.tensor_copy(out=o, in_=a), waits=(cv_free[0],),
                             inc=True)
                t_ev = Sc(lambda e, o=cvbuf[:, 3:515], a=pb[0][:, 0:MT]:
                          e.activation(out=o, in_=a, func=AF.Copy),
                          waits=(fs[j]["hmm"], cv_free[0]), inc=True)
                psum_free[0] = t_ev
                G(lambda e, o=stash[:, 3 * j:3 * j + 3],
                  a=cvbuf[:, 512:515]: e.tensor_copy(out=o, in_=a),
                  waits=(t_ev, t_s0))
                V(lambda e, o=cbuf, a=cvbuf[:, 0:512],
                  w0=convw[:, 4 * j:4 * j + 1], b=convb[:, j:j + 1]:
                  e.tensor_scalar(out=o, in0=a, scalar1=w0, scalar2=b,
                                  op0=OP.mult, op1=OP.add), waits=(t_ev, t_s0))
                V(lambda e, o=bcB, a=cvbuf[:, 1:513],
                  w=convw[:, 4 * j + 1:4 * j + 2]:
                  e.tensor_scalar(out=o, in0=a, scalar1=w, scalar2=None,
                                  op0=OP.mult))
                V(lambda e, o=scr1, a=cvbuf[:, 2:514],
                  w=convw[:, 4 * j + 2:4 * j + 3]:
                  e.tensor_scalar(out=o, in0=a, scalar1=w, scalar2=None,
                                  op0=OP.mult))
                t_d4 = V(lambda e, o=scr2, a=cvbuf[:, 3:515],
                         w=convw[:, 4 * j + 3:4 * j + 4]:
                         e.tensor_scalar(out=o, in0=a, scalar1=w, scalar2=None,
                                         op0=OP.mult), inc=True)
                t_ab_ = V(lambda e, o=cbuf, a=bcB: e.tensor_tensor(
                    out=o, in0=o, in1=a, op=OP.add), waits=(t_d4,), inc=True)
                t_cd_ = V(lambda e, o=scr1, a=scr2: e.tensor_tensor(
                    out=o, in0=o, in1=a, op=OP.add), inc=True)
                t_cv = V(lambda e, o=cbuf, a=scr1: e.tensor_tensor(
                    out=o, in0=o, in1=a, op=OP.add), waits=(t_ab_, t_cd_),
                    inc=True)
                cv_free[0] = t_cv
                fs[j]["cv"] = t_cv

            def em_csilu(j):
                fs[j]["hid"] = Sc(lambda e, o=hidbuf_p[j % 2], a=cbuf:
                                  e.activation(out=o, in_=a, func=AF.Silu),
                                  waits=(fs[j]["cv"],), inc=True)

            def em_tr(j):
                t_ht = None
                for bb in range(4):
                    t_ht = T(lambda e, a=hidbuf_p[j % 2][:, 128 * bb:128 * bb + 128],
                             o=pb[3][:, 128 * bb:128 * bb + 128]:
                             e.transpose(o, a, identf),
                             waits=(fs[j]["hid"], psum_free[3]), inc=(bb == 3))
                tok_hidT = V(lambda e, o=hidTb_p[j % 2], a=pb[3][:, 0:512]:
                             e.tensor_copy(out=o, in_=a), waits=(t_ht,),
                             inc=True)
                psum_free[3] = tok_hidT
                fs[j]["hT"] = tok_hidT

            def em_scan_pre(j, c):
                t_ar = p.dma("gpsimd", arow[c],
                             aflat[2 * j:2 * j + 2, 256 * c:256 * c + 256].unsqueeze(0),
                             waits=(tok_afl, arow_free[c]),
                             sem=f"s_ar{c}")
                t_bc = T(lambda e, r=arow[c]: e.matmul(
                    pb[4][:, 0:512], onesr(64), r, start=True, stop=True),
                    waits=(t_ar,) + tuple(psum_free[4]
                                          if isinstance(psum_free[4], list)
                                          else [psum_free[4]]), inc=True)
                arow_free[c] = t_bc
                psum_free[4] = []
                fs[j]["bc%d" % c] = t_bc

            def em_scan_it(j, c, hh):
                t_bc = fs[j]["bc%d" % c]
                hidTx = hidTb_p[j % 2]
                it = 2 * c + hh
                ip = it % 2
                mdX = mdbuf_p[ip]
                mt0X = mt0b_p[ip]
                mt1X = mt1b_p[ip]
                eeX = eeb_p[ip]
                ceX = ceb_p[ip]
                stX = stsb_p[ip]
                bdX = bdb_p[ip]
                sbank = 3 if ip == 0 else 5
                hcol = 256 * hh
                hg = 2 * j + hh
                acol0 = acumT[:, 64 * (2 * c) + hg:64 * (2 * c) + hg + 1]
                acol1 = acumT[:, 64 * (2 * c + 1) + hg:64 * (2 * c + 1) + hg + 1]
                V(lambda e, o=mdX[:, 0:256], a=pb[4][:, hcol:hcol + 256],
                  s=acol0: e.tensor_scalar(
                      out=o, in0=a, scalar1=s, scalar2=0.0,
                      op0=OP.subtract, op1=OP.min),
                  waits=(t_bc, tok_acumT, prev_iter[ip]))
                V(lambda e, o=mdX[:, 256:384],
                  a=pb[4][:, hcol + 128:hcol + 256], s=acol1:
                  e.tensor_scalar(out=o, in0=a, scalar1=s, scalar2=0.0,
                                  op0=OP.subtract, op1=OP.min))
                t_md = vtok()
                t_me = Sc(lambda e, o=mt0X, a=mdX[:, 0:256]:
                          e.activation(out=o, in_=a, func=AF.Exp),
                          waits=(t_md,), inc=True)
                V(lambda e, o=mt0X, a=mt0X,
                  g=gmm[:, 384 * c:384 * c + 256],
                  s=dtT_tok[:, 64 * (2 * c) + hg:64 * (2 * c) + hg + 1]:
                  e.scalar_tensor_tensor(
                      out=o, in0=a, scalar=s, in1=g, op0=OP.mult,
                      op1=OP.mult),
                  waits=(t_me, tok_gmm, tok_dtT))
                t_me1 = Sc(lambda e, o=mt1X, a=mdX[:, 256:384]:
                           e.activation(out=o, in_=a, func=AF.Exp),
                           inc=True)
                t_mt = V(lambda e, o=mt1X, a=mt1X,
                         g=gmm[:, 384 * c + 256:384 * c + 384],
                         s=dtT_tok[:, 64 * (2 * c + 1) + hg:
                                   64 * (2 * c + 1) + hg + 1]:
                         e.scalar_tensor_tensor(
                             out=o, in0=a, scalar=s, in1=g,
                             op0=OP.mult, op1=OP.mult),
                         waits=(t_me1, tok_dtT), inc=True)
                t_ee = Sc(lambda e, o=eeX, a=pb[4][:, hcol:hcol + 256]:
                          e.activation(out=o, in_=a, func=AF.Exp),
                          waits=(t_bc, prev_iter[ip]), inc=True)
                t_ce = G(lambda e, o=ceX, a=cbT[:, 256 * c:256 * c + 256],
                         b=eeX: e.tensor_tensor(out=o, in0=a, in1=b,
                                                op=OP.mult),
                         waits=(t_ee, tok_cbT), inc=True)
                for sb in range(2):
                    bcol = 64 * (2 * c + sb) + hg
                    V(lambda e, o=bdX[:, 128 * sb:128 * sb + 128],
                      a=btokb[:, 128 * (2 * c + sb):128 * (2 * c + sb) + 128],
                      s=ddtT[:, bcol:bcol + 1]: e.tensor_scalar(
                          out=o, in0=a, scalar1=s, scalar2=None,
                          op0=OP.mult),
                      waits=(tok_btok, tok_ddtT, prev_iter[ip]))
                t_bd = vtok()
                t_st = None
                for sb in range(2):
                    t_st = T(lambda e, l=bdX[:, 128 * sb:128 * sb + 128],
                             r=hidTx[:, 128 * (2 * c + sb) + 64 * hh:
                                     128 * (2 * c + sb) + 64 * hh + 64],
                             sb=sb: e.matmul(pb[sbank][0:128, 0:64], l, r,
                                             start=(sb == 0),
                                             stop=(sb == 1)),
                             waits=(t_bd, fs[j]["hT"], psum_free[sbank]),
                             inc=(sb == 1))
                t_sev = Sc(lambda e, o=stX, a=pb[sbank][0:128, 0:64]:
                           e.activation(out=o, in_=a, func=AF.Copy),
                           waits=(t_st,), inc=True)
                psum_free[sbank] = t_sev
                r0 = 64 * hh
                qb = {(0, 0): 1, (0, 1): 2, (1, 0): 6, (1, 1): 7}[(c, hh)]
                yreg = pb[qb][r0:r0 + 64, 0:256]
                T(lambda e, l=hidTx[:, 128 * (2 * c) + 64 * hh:
                                    128 * (2 * c) + 64 * hh + 64],
                  r=mt0X, o=yreg:
                  e.matmul(o, l, r, start=True, stop=False),
                  waits=(fs[j]["hT"], t_mt, psum_free[qb]))
                T(lambda e, l=hidTx[:, 128 * (2 * c + 1) + 64 * hh:
                                    128 * (2 * c + 1) + 64 * hh + 64],
                  r=mt1X, o=pb[qb][r0:r0 + 64, 128:256]: e.matmul(
                      o, l, r, start=False, stop=False))
                tok_yp = T(lambda e, l=stX, r=ceX, o=yreg:
                           e.matmul(o, l, r, start=False, stop=True),
                           waits=(t_sev, t_ce), inc=True)
                fs[j].setdefault("yparts", []).append(tok_yp)
                prev_iter[ip] = tok_yp
                psum_free[4] = psum_free[4] + [t_ee, t_md]

            def em_epi(j):
                t_hd = V(lambda e, o=hdbuf, a=hidbuf_p[j % 2],
                         s=dcols[:, j:j + 1]:
                         e.tensor_scalar(out=o, in0=a, scalar1=s, scalar2=None,
                                         op0=OP.mult),
                         waits=(fs[j]["hid"], tok_consts, hd_free[0]),
                         inc=True)
                t_y1s = []
                for (cq, hq), qb in {(0, 0): 1, (0, 1): 2, (1, 0): 6,
                                     (1, 1): 7}.items():
                    r0q = 64 * hq
                    t_y1 = V(lambda e, o=hdbuf[r0q:r0q + 64,
                                              256 * cq:256 * cq + 256],
                             a=pb[qb][r0q:r0q + 64, 0:256]:
                             e.tensor_tensor(out=o, in0=a, in1=o, op=OP.add),
                             waits=tuple(fs[j]["yparts"]) + (t_hd,),
                             inc=True)
                    psum_free[qb] = t_y1
                    t_y1s.append(t_y1)
                t_yf = G(lambda e, o=yfb[:, MT * j:MT * j + MT], a=hdbuf,
                         b=sgbuf_p[j % 2]: e.tensor_tensor(
                             out=o, in0=a, in1=b, op=OP.mult),
                         waits=tuple(t_y1s) + (fs[j]["sg"],), inc=True)
                sg_free[j % 2] = t_yf
                hd_free[0] = t_yf
                t_ysq = G(lambda e, o=xsqb, a=yfb[:, MT * j:MT * j + MT]:
                          e.tensor_tensor(out=o, in0=a, in1=a, op=OP.mult),
                          waits=(t_yf, rs_prev[0]), inc=True)
                t_rs = T(lambda e, r=xsqb: e.matmul(
                    pb[5][0:1, 0:MT], onesb_c1, r, start=True, stop=True),
                    waits=(t_ysq, psum_free[5]), inc=True)
                rs_prev[0] = t_rs
                tok_rmsacc[0] = V(
                    lambda e, o=rowreg3[0:1, :], a=pb[5][0:1, 0:MT]:
                    e.tensor_tensor(out=o, in0=o, in1=a, op=OP.add),
                    waits=(t_rs, tok_rmsacc[0]), inc=True)
                psum_free[5] = tok_rmsacc[0]

            em_dma(0)
            em_gate(0)
            em_gsilu(0)
            em_hid(0)
            em_conv(0)
            em_csilu(0)
            em_tr(0)
            if NSL_I > 1:
                em_dma(1)
            for j in range(NSL_I):
                nx = j + 1
                em_scan_pre(j, 0)
                if nx < NSL_I:
                    em_gate(nx)
                em_scan_it(j, 0, 0)
                em_scan_it(j, 0, 1)
                em_scan_pre(j, 1)
                if nx < NSL_I:
                    em_gsilu(nx)
                    if not halo:
                        em_hid(nx)
                em_scan_it(j, 1, 0)
                if nx < NSL_I:
                    if halo:
                        em_hid(nx)
                    em_conv(nx)
                em_scan_it(j, 1, 1)
                if nx < NSL_I:
                    em_csilu(nx)
                    em_tr(nx)
                    if nx + 1 < NSL_I:
                        em_dma(nx + 1)
                em_epi(j)
            t_rmsum = tok_rmsacc[0]
            t_v1 = V(lambda e, o=rowreg3[0:1, :]:
              e.tensor_scalar(out=o, in0=o, scalar1=1.0 / INTER, scalar2=EPS,
                              op0=OP.mult, op1=OP.add), waits=(t_rmsum,),
              inc=True)
            t_s1 = Sc(lambda e, o=rowreg3[0:1, :]: e.activation(
                out=o, in_=o, func=AF.Sqrt), waits=(t_v1,), inc=True)
            t_r1 = V(lambda e, o=rowreg3[0:1, :]: e.reciprocal(out=o, in_=o),
                     waits=(t_s1,), inc=True)
            t_mm = T(lambda e: e.matmul(pb[6][:, 0:MT], onesr(0),
                                        rowreg3[0:1, :], start=True,
                                        stop=True),
                     waits=(t_r1, psum_free[6]), inc=True)
            t_rmsB = V(lambda e, o=bcA, a=pb[6][:, 0:MT]: e.tensor_copy(
                out=o, in_=a), waits=(t_mm,), inc=True)
            psum_free[6] = t_rmsB
            tok_yn = None
            for k in range(NSL_I):
                tyn1 = G(lambda e, a=yfb[:, MT * k:MT * k + MT], o=scr1, b=bcA:
                         e.tensor_tensor(out=o, in0=a, in1=b, op=OP.mult),
                         waits=(t_rmsB, tok_yn), inc=True)
                tok_yn = V(lambda e, o=yfb[:, MT * k:MT * k + MT], a=scr1,
                           s=nwcols[:, k:k + 1]: e.tensor_scalar(
                               out=o, in0=a, scalar1=s, scalar2=None,
                               op0=OP.mult), waits=(tyn1,), inc=True)
            tok_x2 = []
            for ds in range(16):
                par = ds % 2
                t_w = p.dma("gpsimd", wstr[par], w_out_img[ds],
                            waits=(w_free[par],), sem=f"s_w{par}")
                bank = 6 + (ds % 2)
                t_wo = None
                for ic in range(NSL_I):
                    t_wo = T(lambda e, l=wstr[par][:, 128 * ic:128 * ic + 128],
                             r=yfb[:, MT * ic:MT * ic + MT], ic=ic,
                             ob=pb[bank][:, 0:MT]: e.matmul(
                                 ob, l, r, start=(ic == 0),
                                 stop=(ic == NSL_I - 1)),
                             waits=(t_w, tok_yn, psum_free[bank]),
                             inc=(ic == NSL_I - 1))
                w_free[par] = t_wo
                t_x2 = V(lambda e, o=xbuf[:, MT * ds:MT * ds + MT],
                         a=pb[bank][:, 0:MT]: e.tensor_tensor(
                             out=o, in0=a, in1=o, op=OP.add),
                         waits=(t_wo,), inc=True)
                psum_free[bank] = t_x2
                tok_x2.append(t_x2)
            tok_x2_all = tok_x2[-1]
            t_mB2, _ = ln_stats(xbuf, qxb, sqstat, rowreg1[0:1, :],
                                rowreg2[0:1, :], tok_x2_all, False)
            toks = {}
            toks["mxA"] = V(lambda e, o=rmaxb, a=xbuf[:, 0:MT],
                            b=xbuf[:, MT:2 * MT]:
                            e.tensor_tensor(out=o, in0=a, in1=b, op=OP.max),
                            waits=(t_mB2,), inc=True)
            toks["mnA"] = V(lambda e, o=rmnb, a=xbuf[:, 0:MT],
                            b=xbuf[:, MT:2 * MT]:
                            e.tensor_tensor(out=o, in0=a, in1=b, op=OP.min),
                            inc=True)
            toks["mxB"] = V(lambda e, o=scr1, a=xbuf[:, 2 * MT:3 * MT],
                            b=xbuf[:, 3 * MT:4 * MT]:
                            e.tensor_tensor(out=o, in0=a, in1=b, op=OP.max),
                            inc=True)
            toks["mnB"] = V(lambda e, o=scr2, a=xbuf[:, 2 * MT:3 * MT],
                            b=xbuf[:, 3 * MT:4 * MT]:
                            e.tensor_tensor(out=o, in0=a, in1=b, op=OP.min),
                            inc=True)
            for k in range(4, NSL_E):
                kx = "mxA" if (k % 2 == 0) else "mxB"
                kn = "mnA" if (k % 2 == 0) else "mnB"
                dst_mx = rmaxb if (k % 2 == 0) else scr1
                dst_mn = rmnb if (k % 2 == 0) else scr2
                toks[kx] = V(lambda e, o=dst_mx, a=xbuf[:, MT * k:MT * k + MT]:
                             e.tensor_tensor(out=o, in0=a, in1=o, op=OP.max),
                             waits=(toks[kx],), inc=True)
                toks[kn] = V(lambda e, o=dst_mn, a=xbuf[:, MT * k:MT * k + MT]:
                             e.tensor_tensor(out=o, in0=a, in1=o, op=OP.min),
                             waits=(toks[kn],), inc=True)
            t_ac2 = V(lambda e, o=rmaxb, a=scr1: e.tensor_tensor(
                out=o, in0=o, in1=a, op=OP.max),
                waits=(toks["mxA"], toks["mxB"]), inc=True)
            t_ac3 = V(lambda e, o=rmnb, a=scr2: e.tensor_tensor(
                out=o, in0=o, in1=a, op=OP.min),
                waits=(toks["mnA"], toks["mnB"]), inc=True)
            t_ac4 = V(lambda e, o=rmaxb, a=meanB: e.tensor_tensor(
                out=o, in0=o, in1=a, op=OP.subtract), waits=(t_ac2, t_ac3),
                inc=True)
            t_ac5 = V(lambda e, o=rmnb, a=meanB: e.tensor_tensor(
                out=o, in0=a, in1=o, op=OP.subtract), waits=(t_ac3,),
                inc=True)
            t_rmx = V(lambda e, o=rmaxb, a=rmnb: e.tensor_tensor(
                out=o, in0=o, in1=a, op=OP.max), waits=(t_ac4, t_ac5),
                inc=True)

            def rowmax_to_scales(tok_ready, dst_s448, dst_scale, scale_extra_row):
                t_tp = None
                for bb in range(4):
                    t_tp = T(lambda e, a=rmaxb[:, 128 * bb:128 * bb + 128],
                             o=pb[0][:, 128 * bb:128 * bb + 128]:
                             e.transpose(o, a, identf),
                             waits=(tok_ready, psum_free[0]), inc=(bb == 3))
                trds = []
                for bb in range(4):
                    trds.append(V(lambda e, o=scr2[:, bb:bb + 1],
                      a=pb[0][:, 128 * bb:128 * bb + 128]: e.tensor_reduce(
                          out=o, in_=a, axis=AX.X, op=OP.max), waits=(t_tp,),
                      inc=True))
                t_rd2 = vtok(waits=tuple(trds))
                psum_free[0] = t_rd2
                t_tp2 = T(lambda e: e.transpose(pb[0][0:4, 0:128],
                                                scr2[:, 0:4], identf),
                          waits=(t_rd2, psum_free[0]), inc=True)
                t_ev3 = V(lambda e: e.tensor_copy(out=scr2[0:4, 0:128],
                                                  in_=pb[0][0:4, 0:128]),
                          waits=(t_tp2,), inc=True)
                psum_free[0] = t_ev3
                t_dmo = p.dma("gpsimd", row_scr[0:4, :], scr2[0:4, 0:128],
                              waits=(t_ev3, p.tok("s_rs")), sem="s_rs")
                t_dmi = p.dma("gpsimd", rowreg3[0:1, :],
                              row_scr[0:4, :].rearrange("a b -> (a b)").unsqueeze(0),
                              waits=(t_dmo,), sem="s_rs")
                trr = V(lambda e, o=rowreg4[0:1, :], a=rowreg3[0:1, :]:
                  e.reciprocal(out=o, in_=a), waits=(t_dmi,), inc=True)
                t_s448r = V(lambda e, o=rowreg4[0:1, :]: e.tensor_scalar(
                    out=o, in0=o, scalar1=448.0, scalar2=None, op0=OP.mult),
                    waits=(trr,), inc=True)
                t_m448r = V(lambda e, o=rowreg3[0:1, :]: e.tensor_scalar(
                    out=o, in0=o, scalar1=1.0 / 448.0, scalar2=None,
                    op0=OP.mult), waits=(t_dmi,), inc=True)
                if scale_extra_row is not None:
                    t_m448r = V(lambda e, o=rowreg3[0:1, :], a=scale_extra_row:
                                e.tensor_tensor(out=o, in0=o, in1=a,
                                                op=OP.mult),
                                waits=(t_m448r,), inc=True)
                t_sclr = t_m448r
                t_k1 = T(lambda e: e.matmul(pb[0][:, 0:MT], onesr(0),
                                            rowreg4[0:1, :], start=True,
                                            stop=True),
                         waits=(t_s448r, psum_free[0]), inc=True)
                t_b1_ = V(lambda e, o=dst_s448: e.tensor_copy(
                    out=o, in_=pb[0][:, 0:MT]), waits=(t_k1,), inc=True)
                psum_free[0] = t_b1_
                t_k2 = T(lambda e: e.matmul(pb[0][:, 0:MT], onesr(0),
                                            rowreg3[0:1, :], start=True,
                                            stop=True),
                         waits=(t_sclr, psum_free[0]), inc=True)
                t_b2_ = V(lambda e, o=dst_scale: e.tensor_copy(
                    out=o, in_=pb[0][:, 0:MT]), waits=(t_k2,), inc=True)
                psum_free[0] = t_b2_
                return t_b1_, t_b2_

            t_s448, t_m448 = rowmax_to_scales(t_rmx, bcA, bcB,
                                              rowreg2[0:1, :])
            tok_qx = None
            for k in range(NSL_E):
                tq1 = G(lambda e, a=xbuf[:, MT * k:MT * k + MT], o=scr2:
                        e.tensor_tensor(out=o, in0=a, in1=meanB,
                                        op=OP.subtract),
                        waits=(t_s448, tok_qx), inc=True)
                tq2 = G(lambda e, o=scr1, a=scr2, b=bcA: e.tensor_tensor(
                    out=o, in0=a, in1=b, op=OP.mult), waits=(tq1,), inc=True)
                tok_qx = emit_quant(scr1, qxb[:, MT * k:MT * k + MT], MT,
                                    vwait=tq2)
            gtoks = {}
            prev_g3 = None
            prev_t = None
            V(lambda e: e.memset(rmaxb, 0.0))
            V(lambda e: e.memset(rmnb, 0.0))
            V(lambda e: e.memset(bcA2, 0.0))
            t_rm0b = V(lambda e: e.memset(bcB2, 0.0), inc=True)
            for r in range(NSL_R):
                parA = r % 2
                parB = 2 + r % 2
                t_wgq = p.dma("sync", wstr[parA][:, 0:2048], qg_scr[r],
                              waits=(w_free[parA], tok_p0a, tok_p0b),
                              sem=f"s_g{parA}")
                t_wuq = p.dma("scalar", wstr[parB], qu_scr[r],
                              waits=(w_free[parB], tok_p0a, tok_p0b),
                              sem=f"s_g{parB}")
                t_gq = None
                for k in range(NSL_E):
                    t_gq = T(lambda e, l=wstr[parA][:, 128 * k:128 * k + 128],
                             r_=qxb[:, MT * k:MT * k + MT], k=k: e.matmul(
                                 pb[0][:, 0:MT], l, r_, start=(k == 0),
                                 stop=(k == NSL_E - 1)),
                             waits=(t_wgq, tok_qx, psum_free[0]),
                             inc=(k == NSL_E - 1))
                w_free[parA] = t_gq
                t_uq = None
                for k in range(NSL_E):
                    t_uq = T(lambda e, l=wstr[parB][:, 128 * k:128 * k + 128],
                             r_=qxb[:, MT * k:MT * k + MT], k=k: e.matmul(
                                 pb[1][:, 0:MT], l, r_, start=(k == 0),
                                 stop=(k == NSL_E - 1)),
                             waits=(t_wuq, psum_free[1]),
                             inc=(k == NSL_E - 1))
                w_free[parB] = t_uq
                t_g2 = V(lambda e, o=scr1, a=pb[0][:, 0:MT],
                         s=wsg_cols[:, r:r + 1], b=bcB: e.scalar_tensor_tensor(
                             out=o, in0=a, scalar=s, in1=b, op0=OP.mult,
                             op1=OP.mult),
                         waits=(t_gq, t_m448, prev_g3, prev_t), inc=True)
                psum_free[0] = t_g2
                t_g3 = Sc(lambda e, o=sgbuf, a=scr1: e.activation(
                    out=o, in_=a, func=AF.Silu), waits=(t_g2,), inc=True)
                prev_g3 = t_g3
                t_u2 = V(lambda e, o=scr2, a=pb[1][:, 0:MT],
                         s=wsu_cols[:, r:r + 1], b=bcB: e.scalar_tensor_tensor(
                             out=o, in0=a, scalar=s, in1=b, op0=OP.mult,
                             op1=OP.mult),
                         waits=(t_uq, t_m448, prev_t), inc=True)
                psum_free[1] = t_u2
                t_t = G(lambda e, o=tqt[:, MT * r:MT * r + MT], a=sgbuf,
                        b=scr2: e.tensor_tensor(out=o, in0=a, in1=b,
                                                op=OP.mult),
                        waits=(t_g3, t_u2, t_rm0b), inc=True)
                prev_t = t_t
                kx = "gmxA" if (r % 2 == 0) else "gmxB"
                kn = "gmnA" if (r % 2 == 0) else "gmnB"
                dst_mx = rmaxb if (r % 2 == 0) else bcA2
                dst_mn = rmnb if (r % 2 == 0) else bcB2
                gtoks[kx] = V(lambda e, o=dst_mx, a=tqt[:, MT * r:MT * r + MT]:
                              e.tensor_tensor(out=o, in0=a, in1=o, op=OP.max),
                              waits=(t_t, gtoks.get(kx)), inc=True)
                gtoks[kn] = V(lambda e, o=dst_mn, a=tqt[:, MT * r:MT * r + MT]:
                              e.tensor_tensor(out=o, in0=a, in1=o, op=OP.min),
                              waits=(gtoks.get(kn),), inc=True)
            tg2 = V(lambda e, o=rmaxb, a=bcA2: e.tensor_tensor(
                out=o, in0=o, in1=a, op=OP.max),
                waits=(gtoks["gmxA"], gtoks["gmxB"]), inc=True)
            tg3 = V(lambda e, o=rmnb, a=bcB2: e.tensor_tensor(
                out=o, in0=o, in1=a, op=OP.min),
                waits=(gtoks["gmnA"], gtoks["gmnB"]), inc=True)
            tg4 = V(lambda e, o=rmnb: e.tensor_scalar(
                out=o, in0=o, scalar1=-1.0, scalar2=None, op0=OP.mult),
                waits=(tg2, tg3), inc=True)
            t_tmax = V(lambda e, o=rmaxb, a=rmnb: e.tensor_tensor(
                out=o, in0=o, in1=a, op=OP.max), waits=(tg4,), inc=True)
            t_st448, t_mt448 = rowmax_to_scales(t_tmax, bcA, bcB, None)
            tok_qt = None
            for r in range(NSL_R):
                tv = G(lambda e, a=tqt[:, MT * r:MT * r + MT], o=scr1, b=bcA:
                       e.tensor_tensor(out=o, in0=a, in1=b, op=OP.mult),
                       waits=(t_st448, tok_qt), inc=True)
                tok_qt = emit_quant(scr1, tqt[:, MT * r:MT * r + MT], MT,
                                    vwait=tv)
            out_slab_free = [None, None]
            for hg in range(2):
                t_dn = None
                for ms in range(NSL_R):
                    par = ms % 2
                    t_wd = p.dma("sync", wstr[par][:, 0:1024],
                                 qd_scr[ms][:, 1024 * hg:1024 * hg + 1024],
                                 waits=(w_free[par],), sem=f"s_g{par}")
                    for hc in range(8):
                        t_dn = T(lambda e, l=wstr[par][:, 128 * hc:128 * hc + 128],
                                 r_=tqt[:, MT * ms:MT * ms + MT], hc=hc, ms=ms:
                                 e.matmul(pb[hc][:, 0:MT], l, r_,
                                          start=(ms == 0),
                                          stop=(ms == NSL_R - 1)),
                                 waits=((t_wd, tok_qt) +
                                        (tuple(psum_free.values())
                                         if ms == 0 else ())),
                                 inc=(hc == 7))
                    w_free[par] = t_dn
                for hc in range(8):
                    hs = 8 * hg + hc
                    par = hc % 2
                    osl = scr1 if par == 0 else scr2
                    tdv = V(lambda e, o=osl, a=pb[hc][:, 0:MT],
                            s=wsd_cols[:, hs:hs + 1], b=bcB:
                            e.scalar_tensor_tensor(
                                out=o, in0=a, scalar=s, in1=b, op0=OP.mult,
                                op1=OP.mult),
                            waits=(t_dn, t_mt448, out_slab_free[par]),
                            inc=True)
                    t_os = V(lambda e, o=osl, a=xbuf[:, MT * hs:MT * hs + MT]:
                             e.tensor_tensor(out=o, in0=o, in1=a, op=OP.add),
                             waits=(tdv,), inc=True)
                    psum_free[hc] = t_os
                    t_do = p.dma("gpsimd",
                                 outT[128 * hs:128 * hs + 128, MT * m:MT * m + MT],
                                 osl, waits=(t_os,),
                                 sem=("s_st0" if par == 0 else "s_st1"))
                    out_slab_free[par] = t_do
                    out_final = t_do
            tok_x_free = [p.tok("vx"), out_slab_free[0], out_slab_free[1]]

        p.emit("sync", lambda e: e.nop(), waits=(out_final,))

        with nc.Block() as block:
            def mk(ename):
                def body(eng):
                    for (rw, fn, inc, amt) in p.items[ename]:
                        for (s, v) in rw:
                            eng.wait_ge(semh[s], v)
                        inst = fn(eng)
                        if inc is not None:
                            inst.then_inc(semh[inc], amt)
                return body
            block.sync(mk("sync"))
            block.vector(mk("vector"))
            block.scalar(mk("scalar"))
            block.tensor(mk("tensor"))
            block.gpsimd(mk("gpsimd"))
    return nc


_NC_CACHE = [None]


def kernel(**inputs):
    x = np.asarray(inputs["L0_attn_input_states"], np.float32)
    W_in = np.asarray(inputs["L0_attn_W_in"], np.float32)
    conv_w = np.asarray(inputs["L0_attn_conv_w"], np.float32)
    conv_b = np.asarray(inputs["L0_attn_conv_b"], np.float32)
    dt_bias = np.asarray(inputs["L0_attn_dt_bias"], np.float32)
    A_log = np.asarray(inputs["L0_attn_A_log"], np.float32)
    D = np.asarray(inputs["L0_attn_D"], np.float32)
    norm_w = np.asarray(inputs["L0_attn_norm_w"], np.float32)
    W_out = np.asarray(inputs["L0_attn_W_out"], np.float32)
    w_gate = np.asarray(inputs["L0_ffn_w_gate"], np.float32)
    w_up = np.asarray(inputs["L0_ffn_w_up"], np.float32)
    w_down = np.asarray(inputs["L0_ffn_w_down"], np.float32)
    dn_idx = np.asarray(inputs["L0_ffn_down_reorder_index"])

    def img_lhsT(wT, nsl_out, kchunks):
        K, O = wT.shape
        a = wT.reshape(kchunks, 128, nsl_out, 128)
        return np.ascontiguousarray(a.transpose(2, 1, 0, 3).reshape(nsl_out, 128, kchunks * 128))

    W_inT = np.zeros((E, 67 * 128), np.float32)
    W_inT[:, 0:8512] = W_in.T
    w_in_img = img_lhsT(W_inT, 67, 16)
    w_out_img = img_lhsT(np.ascontiguousarray(W_out.T), 16, 32)
    wg_img = img_lhsT(np.ascontiguousarray(w_gate.T), NSL_R, 16)
    wu_img = img_lhsT(np.ascontiguousarray(w_up.T), NSL_R, 16)
    dinv = np.argsort(dn_idx)
    w2 = np.ascontiguousarray(w_down[:, dinv])
    wd_img = img_lhsT(np.ascontiguousarray(w2.T), 16, 44)
    wdT = np.ascontiguousarray(w2.T)
    wd_img = np.ascontiguousarray(wdT.reshape(44, 128, 16, 128).reshape(44, 128, 2048))
    wd_nat = np.ascontiguousarray(w_down.reshape(16, 128, FFI))

    conv_w_i = np.ascontiguousarray(
        conv_w.reshape(NSL_C, 128, 4).transpose(1, 0, 2).reshape(128, NSL_C * 4))
    conv_b_i = np.ascontiguousarray(conv_b.reshape(NSL_C, 128).T)
    dt_bias_i = np.ascontiguousarray(dt_bias.reshape(64, 1))
    a_log_i = np.ascontiguousarray(A_log.reshape(64, 1))
    d_cols_i = np.ascontiguousarray(np.repeat(D, 64).reshape(32, 128).T)
    nw_cols_i = np.ascontiguousarray(norm_w.reshape(32, 128).T)
    tri_i = (np.arange(128)[None, :] >= np.arange(128)[:, None]).astype(np.float32)
    ident_i = np.eye(128, dtype=np.float32)

    shared = dict(w_in_img=w_in_img, w_out_img=w_out_img, wg_img=wg_img,
                  wu_img=wu_img, wd_img=wd_img, wd_nat=wd_nat,
                  conv_w_i=conv_w_i, conv_b_i=conv_b_i, dt_bias_i=dt_bias_i,
                  a_log_i=a_log_i, d_cols_i=d_cols_i, nw_cols_i=nw_cols_i,
                  tri_i=tri_i, ident_i=ident_i)

    in_maps = []
    for core in range(8):
        b = core // 2
        h = core % 2
        t0 = h * TC
        xc = x[b, t0:t0 + TC]
        halo = x[b, t0 - 3:t0] if h == 1 else np.zeros((3, E), np.float32)
        xTc = np.ascontiguousarray(
            np.concatenate([halo, xc], 0).T)
        im = {"xT": xTc}
        im.update(shared)
        in_maps.append(im)

    if _NC_CACHE[0] is None:
        nc = build_nc()
        mybir.codegen_inst_isa_subclasses(nc)
        _NC_CACHE[0] = nc
    nc = _NC_CACHE[0]
    res = run_bass_kernel_spmd(nc, in_maps, core_ids=list(range(8)))
    out = np.empty((4, 4096, E), np.float32)
    for core in range(8):
        b = core // 2
        h = core % 2
        t0 = h * TC
        out[b, t0:t0 + TC, :] = res.results[core]["outT"].T
    return out


